# version 01
# baseline (speedup 1.0000x reference)
import numpy as np
import ml_dtypes
from contextlib import ExitStack

import concourse.bass as bass
import concourse.mybir as mybir
from concourse.bass_utils import run_bass_kernel_spmd

F32 = mybir.dt.float32
BF16 = mybir.dt.bfloat16
AF = mybir.ActivationFunctionType
ALU = mybir.AluOpType
AX = mybir.AxisListType
BF = ml_dtypes.bfloat16

S = 2048
D = 1024
NT = S // 128
ND = D // 128
NCORES = 8
HQ, HKV, HD = 16, 4, 64
E, HID = 8, 2048
CAP = 2048.0
NEG = -1.0e30


class Ctx:

    def __init__(self, nc, es):
        self.nc = nc
        self.es = es
        self.sems = {}
        self.cnt = {}

    def sem(self, name):
        if name not in self.sems:
            self.sems[name] = self.es.enter_context(self.nc.semaphore(name))
            self.cnt[name] = 0
        return self.sems[name]

    def inc(self, instr, name, k=1):
        h = self.sem(name)
        instr.then_inc(h, k)
        self.cnt[name] += k
        return (name, self.cnt[name])

    def dinc(self, instr, name):
        return self.inc(instr, name, 16)

    def wait(self, eng, tok):
        if tok is None:
            return
        name, val = tok
        eng.wait_ge(self.sems[name], val)

    def tok(self, name):
        return (name, self.cnt[name])


DSY, DGP, DSC = "dsy", "dgp", "dsc"
TQ, VQ, SQ, GQ = "tq", "vq", "sq", "gq"


def _stream_weight(nc, cx, dram_ap, stag, wbuf, row0, nrows, col0, ncols,
                   cast_eng, cast_q, prev_war=(), chan="sw"):
    if prev_war is None:
        prev_war = ()
    if prev_war and isinstance(prev_war[0], str):
        prev_war = (prev_war,)
    """Stream rows [row0,row0+nrows) x cols [col0,col0+ncols) of a f32 DRAM
    matrix into wbuf bf16 ([128, ntiles*ncols], tile r at cols r*ncols).
    stag: [128, 2*ncols] f32 ping-pong.  Loads on sync queue; casts on
    cast_eng (token stream cast_q).  prev_war: token guarding wbuf reuse.
    Returns last cast token."""
    R = nrows // 128
    casts = []
    for r in range(R):
        p = r % 2
        if r >= 2:
            cx.wait(nc.sync, casts[r - 2])
        if r < 2:
            for pw in prev_war:
                cx.wait(nc.sync, pw)
        t = cx.dinc(nc.sync.dma_start(
            out=stag[:, p * ncols:(p + 1) * ncols],
            in_=dram_ap[row0 + r * 128:row0 + (r + 1) * 128, col0:col0 + ncols]),
            chan + str(p))
        cx.wait(cast_eng, t)
        casts.append(cx.inc(cast_eng.tensor_copy(
            wbuf[:, r * ncols:(r + 1) * ncols],
            stag[:, p * ncols:(p + 1) * ncols]), cast_q))
    return casts[-1]


def build_launch1(debug=False):
    nc = bass.Bass("TRN2", target_bir_lowering=False)

    x_d = nc.dram_tensor("x", [S, D], F32, kind="ExternalInput").ap()
    wqT_d = nc.dram_tensor("wqT", [D, D], F32, kind="ExternalInput").ap()
    wkT_d = nc.dram_tensor("wkT", [D, D], F32, kind="ExternalInput").ap()
    wvT_d = nc.dram_tensor("wvT", [D, D], F32, kind="ExternalInput").ap()
    woT_d = nc.dram_tensor("woT", [D, D], F32, kind="ExternalInput").ap()
    bq_d = nc.dram_tensor("bq128", [128, ND], F32, kind="ExternalInput").ap()
    bk_d = nc.dram_tensor("bk128", [128, ND], F32, kind="ExternalInput").ap()
    bvbc_d = nc.dram_tensor("bv_bc", [128, D], F32, kind="ExternalInput").ap()
    bobc_d = nc.dram_tensor("bo_bc", [128, D], F32, kind="ExternalInput").ap()
    g1bc_d = nc.dram_tensor("g1_bc", [128, D], F32, kind="ExternalInput").ap()
    g2bc_d = nc.dram_tensor("g2_bc", [128, D], F32, kind="ExternalInput").ap()
    fwq_d = nc.dram_tensor("fwq", [D, D], F32, kind="ExternalInput").ap()
    fwk_d = nc.dram_tensor("fwk", [D, HKV * HD], F32, kind="ExternalInput").ap()
    fwv_d = nc.dram_tensor("fwv", [D, HKV * HD], F32, kind="ExternalInput").ap()
    fwo_d = nc.dram_tensor("fwo", [D, D], F32, kind="ExternalInput").ap()
    cosb_d = nc.dram_tensor("cosb", [128, S], BF16, kind="ExternalInput").ap()
    sinb_d = nc.dram_tensor("sinb", [128, S], BF16, kind="ExternalInput").ap()
    identb_d = nc.dram_tensor("identb", [128, 128], BF16, kind="ExternalInput").ap()
    identf_d = nc.dram_tensor("identf", [128, 128], F32, kind="ExternalInput").ap()
    ident8_d = nc.dram_tensor("ident8", [8, 8], F32, kind="ExternalInput").ap()
    cmask_d = nc.dram_tensor("cmask", [128, 128], F32, kind="ExternalInput").ap()
    triu_d = nc.dram_tensor("triu", [128, 128], F32, kind="ExternalInput").ap()
    iota8_d = nc.dram_tensor("iota8", [128, 8], F32, kind="ExternalInput").ap()
    iom9_d = nc.dram_tensor("iota8m999", [128, 8], F32, kind="ExternalInput").ap()
    onesb_d = nc.dram_tensor("ones_bf", [128, 1], BF16, kind="ExternalInput").ap()
    ones1_d = nc.dram_tensor("ones1", [1, 128], F32, kind="ExternalInput").ap()
    onesc_d = nc.dram_tensor("ones128c", [128, 1], F32, kind="ExternalInput").ap()
    rw_d = nc.dram_tensor("rw", [D, E], F32, kind="ExternalInput").ap()
    rb_d = nc.dram_tensor("rb8", [E, 1], F32, kind="ExternalInput").ap()

    o_s = nc.dram_tensor("o_s", [S, D], BF16, kind="ExternalOutput").ap()
    o_h2T = nc.dram_tensor("o_h2T", [D, S], BF16, kind="ExternalOutput").ap()
    o_idx0 = nc.dram_tensor("o_idx0", [128, NT], F32, kind="ExternalOutput").ap()
    o_idx1 = nc.dram_tensor("o_idx1", [128, NT], F32, kind="ExternalOutput").ap()
    o_sc0 = nc.dram_tensor("o_sc0", [128, NT], F32, kind="ExternalOutput").ap()
    o_sc1 = nc.dram_tensor("o_sc1", [128, NT], F32, kind="ExternalOutput").ap()
    o_cs0 = nc.dram_tensor("o_csel0", [128, NT], F32, kind="ExternalOutput").ap()
    o_cs01 = nc.dram_tensor("o_csel01", [128, NT], F32, kind="ExternalOutput").ap()
    o_cnt = nc.dram_tensor("o_cnt", [2, E], F32, kind="ExternalOutput").ap()
    if debug:
        o_x1 = nc.dram_tensor("o_x1", [128, NT * D], BF16, kind="ExternalOutput").ap()
        o_QT = nc.dram_tensor("o_QT", [128, ND * S], BF16, kind="ExternalOutput").ap()
        o_KT = nc.dram_tensor("o_KT", [128, ND * S], BF16, kind="ExternalOutput").ap()
        o_V = nc.dram_tensor("o_V", [128, NT * D], BF16, kind="ExternalOutput").ap()
        o_xnT = nc.dram_tensor("o_xnT", [128, ND * S], BF16, kind="ExternalOutput").ap()
        o_kT2 = nc.dram_tensor("o_kT2", [128, 2 * S], BF16, kind="ExternalOutput").ap()
        o_v2 = nc.dram_tensor("o_v2", [128, NT * 256], BF16, kind="ExternalOutput").ap()
        o_qT2h = nc.dram_tensor("o_qT2h", [128, ND * 1024], BF16, kind="ExternalOutput").ap()
        o_wcb = nc.dram_tensor("o_wcb", [128, 4096], BF16, kind="ExternalOutput").ap()
        o_w0 = nc.dram_tensor("o_w0", [128, NT * 128], BF16, kind="ExternalOutput").ap()
        o_ob0 = nc.dram_tensor("o_ob0", [128, D], BF16, kind="ExternalOutput").ap()
        o_pre0 = nc.dram_tensor("o_pre0", [128, D], F32, kind="ExternalOutput").ap()

    es = ExitStack()
    cx = Ctx(nc, es)

    def sb(name, p, c, dt=F32):
        return es.enter_context(nc.sbuf_tensor(name, [p, c], dt))

    c_identb = sb("c_identb", 128, 128, BF16)
    c_identf = sb("c_identf", 128, 128, F32)
    c_ident8 = sb("c_ident8", 8, 8, F32)
    c_cmask = sb("c_cmask", 128, 128, F32)
    c_triu = sb("c_triu", 128, 128, F32)
    c_iota8 = sb("c_iota8", 128, 8, F32)
    c_iom9 = sb("c_iom9", 128, 8, F32)
    c_onesb = sb("c_onesb", 128, 1, BF16)
    c_ones1 = sb("c_ones1", 1, 128, F32)
    c_onesc = sb("c_onesc", 128, 1, F32)
    c_bq = sb("c_bq", 128, ND, F32)
    c_bk = sb("c_bk", 128, ND, F32)
    c_bvbc = sb("c_bvbc", 128, D, F32)
    c_bobc = sb("c_bobc", 128, D, F32)
    c_g1bc = sb("c_g1bc", 128, D, F32)
    c_g2bc = sb("c_g2bc", 128, D, F32)
    c_rw = sb("c_rw", 128, ND * E, F32)
    c_rb = sb("c_rb", 8, 1, F32)

    sb_x1 = sb("sb_x1", 128, NT * D, BF16)
    sb_lgtok = sb("sb_lgtok", 128, NT * E, F32)
    sb_idx0 = sb("sb_idx0", 128, NT, F32)
    sb_idx1 = sb("sb_idx1", 128, NT, F32)
    sb_sc0 = sb("sb_sc0", 128, NT, F32)
    sb_sc1 = sb("sb_sc1", 128, NT, F32)
    sb_cs0 = sb("sb_cs0", 128, NT, F32)
    sb_cs01 = sb("sb_cs01", 128, NT, F32)
    sb_run0 = sb("sb_run0", 1, E, F32)
    sb_run1 = sb("sb_run1", 1, E, F32)
    sb_run01 = sb("sb_run01", 1, E, F32)
    st_ = sb("sb_stats", 128, 64, F32)
    c_eps5 = sb("c_eps5", 128, 1, F32)
    c_eps6 = sb("c_eps6", 128, 1, F32)

    store_toks = []

    with es:
        with nc.Block():
            for dst, src in [(c_identb, identb_d), (c_identf, identf_d),
                             (c_ident8, ident8_d),
                             (c_cmask, cmask_d), (c_triu, triu_d),
                             (c_iota8, iota8_d), (c_iom9, iom9_d),
                             (c_onesb, onesb_d), (c_ones1, ones1_d),
                             (c_onesc, onesc_d), (c_bq, bq_d), (c_bk, bk_d),
                             (c_bvbc, bvbc_d), (c_bobc, bobc_d),
                             (c_g1bc, g1bc_d), (c_g2bc, g2bc_d), (c_rb, rb_d)]:
                cx.dinc(nc.sync.dma_start(out=dst[:], in_=src), DSY)
            for r in range(ND):
                cx.dinc(nc.sync.dma_start(out=c_rw[:, r * E:(r + 1) * E],
                                          in_=rw_d[r * 128:(r + 1) * 128, :]), DSY)
            consts_tok = cx.tok(DSY)
            for eng in (nc.vector, nc.scalar, nc.tensor, nc.gpsimd):
                cx.wait(eng, consts_tok)

            nc.vector.memset(c_eps5[:], 1.0e-5)
            nc.vector.memset(c_eps6[:], 1.0e-6)
            nc.vector.memset(sb_run0[:], 0.0)
            nc.vector.memset(sb_run1[:], 0.0)
            nc.vector.memset(sb_run01[:], 0.0)

            g = {}

            def vw(instr):
                tok = cx.inc(instr, VQ)
                cx.wait(nc.vector, tok)
                return tok

            with ExitStack() as aes:
                def asb(name, p, c, dt=F32):
                    return aes.enter_context(nc.sbuf_tensor(name, [p, c], dt))

                sb_QT = asb("sb_QT", 128, ND * S, BF16)
                sb_KT = asb("sb_KT", 128, ND * S, BF16)
                sb_V = asb("sb_V", 128, NT * D, BF16)
                stag = asb("stag", 128, 1024, F32)

                with ExitStack() as a1:
                    sb_xT = a1.enter_context(nc.sbuf_tensor("sb_xT", [128, ND * S], BF16))
                    p_trb = a1.enter_context(nc.psum_tensor("p_trb", [128, 512], BF16))
                    p_trb2 = a1.enter_context(nc.psum_tensor("p_trb2", [128, 512], BF16))
                    p_mm = a1.enter_context(nc.psum_tensor("p_mm", [128, 512], F32))
                    p_mm2 = a1.enter_context(nc.psum_tensor("p_mm2", [128, 512], F32))


                    xtes = ExitStack()
                    xstag = xtes.enter_context(nc.sbuf_tensor("xstag", [128, 2 * D], F32))
                    stagb = xtes.enter_context(nc.sbuf_tensor("stagb", [128, 2 * D], BF16))
                    for tt in range(NT):
                        p = tt % 2
                        if tt >= 2:
                            cx.wait(nc.sync, g[("xcast", tt - 2)])
                        dt_ = cx.dinc(nc.sync.dma_start(
                            out=xstag[:, p * D:(p + 1) * D],
                            in_=x_d[tt * 128:(tt + 1) * 128, :]), "xst%d" % p)
                        if tt >= 2:
                            cx.wait(nc.vector, g[("xtr", tt - 2, 1)])
                        cx.wait(nc.vector, dt_)
                        ct = cx.inc(nc.vector.tensor_copy(
                            stagb[:, p * D:(p + 1) * D],
                            xstag[:, p * D:(p + 1) * D]), VQ)
                        g[("xcast", tt)] = ct
                        for gi in range(2):
                            pt = p_trb if (tt * 2 + gi) % 2 == 0 else p_trb2
                            if tt * 2 + gi >= 2:
                                cx.wait(nc.tensor, g[("xtrc", (tt * 2 + gi) % 2)])
                            cx.wait(nc.tensor, ct)
                            mt = None
                            for j in range(4):
                                d = gi * 4 + j
                                mt = nc.tensor.transpose(
                                    pt[:, j * 128:(j + 1) * 128],
                                    stagb[:, p * D + d * 128:p * D + (d + 1) * 128],
                                    c_identb[:])
                            mtok = cx.inc(mt, TQ)
                            g[("xtr", tt, gi)] = mtok
                            cx.wait(nc.scalar, mtok)
                            sctk = None
                            for j in range(4):
                                d = gi * 4 + j
                                sctk = nc.scalar.copy(
                                    sb_xT[:, d * S + tt * 128:d * S + (tt + 1) * 128],
                                    pt[:, j * 128:(j + 1) * 128])
                            g[("xtrc", (tt * 2 + gi) % 2)] = cx.inc(sctk, SQ)
                    xT_done = cx.tok(SQ)

                    wcb = xtes.enter_context(nc.sbuf_tensor("wcb", [128, 4096], BF16))
                    prev_cast = [None]

                    def load_colblock(dram, col0, ncols, base, war_tok):
                        casts = []
                        for d in range(ND):
                            pp = d % 2
                            if d >= 2:
                                cx.wait(nc.sync, casts[d - 2])
                            else:
                                if prev_cast[0] is not None:
                                    cx.wait(nc.sync, prev_cast[0])
                                for wt_ in (war_tok or []):
                                    cx.wait(nc.sync, wt_)
                            t = cx.dinc(nc.sync.dma_start(
                                out=stag[:, pp * ncols:(pp + 1) * ncols],
                                in_=dram[d * 128:(d + 1) * 128,
                                         col0:col0 + ncols]), "cb%d" % pp)
                            cx.wait(nc.vector, t)
                            if d == 0:
                                for wt_ in (war_tok or []):
                                    cx.wait(nc.vector, wt_)
                            casts.append(cx.inc(nc.vector.tensor_copy(
                                wcb[:, base + d * ncols:base + (d + 1) * ncols],
                                stag[:, pp * ncols:(pp + 1) * ncols]), VQ))
                        prev_cast[0] = casts[-1]
                        return casts[-1]

                    stag_war = [g[("xcast", NT - 1)]]
                    for proj, (outb, bias, dram) in enumerate(
                            [(sb_QT, c_bq, wqT_d), (sb_KT, c_bk, wkT_d)]):
                        for m in range(ND):
                            slot = m % 2
                            war = [g[("wcb", slot)]] if ("wcb", slot) in g else None
                            wtok = load_colblock(dram, m * 128, 128, slot * 1024,
                                                 (war or []) + stag_war
                                                 if (war or stag_war) else None)
                            stag_war = []
                            cx.wait(nc.tensor, wtok)
                            if proj == 0 and m == 0:
                                cx.wait(nc.tensor, xT_done)
                            for nch in range(4):
                                kk = (m * 4 + nch) % 2
                                pm = p_mm if kk == 0 else p_mm2
                                if ("qke", kk) in g:
                                    cx.wait(nc.tensor, g[("qke", kk)])
                                mt = None
                                for d in range(ND):
                                    mt = nc.tensor.matmul(
                                        pm[:],
                                        wcb[:, slot * 1024 + d * 128:
                                            slot * 1024 + (d + 1) * 128],
                                        sb_xT[:, d * S + nch * 512:d * S + (nch + 1) * 512],
                                        start=(d == 0), stop=(d == ND - 1))
                                mtok = cx.inc(mt, TQ)
                                if nch == 3:
                                    g[("wcb", slot)] = mtok
                                cx.wait(nc.scalar, mtok)
                                stk = nc.scalar.activation(
                                    outb[:, m * S + nch * 512:m * S + (nch + 1) * 512],
                                    pm[:], AF.Identity, bias=bias[:, m:m + 1], scale=1.0)
                                g[("qke", kk)] = cx.inc(stk, SQ)
                    for nch in range(2):
                        war = [g[k] for k in (("wcb", 0), ("wcb", 1), ("vwcb",))
                               if k in g]
                        wtok = load_colblock(wvT_d, nch * 512, 512, 0, war)
                        cx.wait(nc.tensor, wtok)
                        if debug and nch == 1:
                            cx.wait(nc.gpsimd, wtok)
                            store_toks.append(cx.dinc(nc.gpsimd.dma_start(
                                out=o_wcb, in_=wcb[:]), DGP))
                        for tt in range(NT):
                            kk = tt % 2
                            pm = p_mm if kk == 0 else p_mm2
                            if ("ve", kk) in g:
                                cx.wait(nc.tensor, g[("ve", kk)])
                            elif ("qke", kk) in g:
                                cx.wait(nc.tensor, g.pop(("qke", kk)))
                            mt = None
                            for d in range(ND):
                                mt = nc.tensor.matmul(
                                    pm[:],
                                    sb_xT[:, d * S + tt * 128:d * S + (tt + 1) * 128],
                                    wcb[:, d * 512:(d + 1) * 512],
                                    start=(d == 0), stop=(d == ND - 1))
                            mtok = cx.inc(mt, TQ)
                            if tt == NT - 1:
                                g[("vwcb",)] = mtok
                            cx.wait(nc.vector, mtok)
                            vtk = nc.vector.tensor_tensor(
                                sb_V[:, tt * D + nch * 512:tt * D + (nch + 1) * 512],
                                pm[:], c_bvbc[:, nch * 512:(nch + 1) * 512], ALU.add)
                            g[("ve", kk)] = cx.inc(vtk, VQ)
                    qkv_done_v = cx.tok(VQ)
                    qkv_done_s = cx.tok(SQ)
                    xtes.close()
                    if debug:
                        cx.wait(nc.gpsimd, qkv_done_v)
                        cx.wait(nc.gpsimd, qkv_done_s)
                        store_toks.append(cx.dinc(nc.gpsimd.dma_start(out=o_QT, in_=sb_QT[:]), "dbg"))
                        store_toks.append(cx.dinc(nc.gpsimd.dma_start(out=o_KT, in_=sb_KT[:]), "dbg"))
                        store_toks.append(cx.dinc(nc.gpsimd.dma_start(out=o_V, in_=sb_V[:]), "dbg"))

                with ExitStack() as a2:
                    def a2sb(name, p, c, dt=F32):
                        return a2.enter_context(nc.sbuf_tensor(name, [p, c], dt))
                    p_s = a2.enter_context(nc.psum_tensor("p_s", [128, 512], F32))
                    p_s2 = a2.enter_context(nc.psum_tensor("p_s2", [128, 512], F32))
                    p_tr = a2.enter_context(nc.psum_tensor("p_tr", [128, 512], BF16))
                    p_tr2 = a2.enter_context(nc.psum_tensor("p_tr2", [128, 512], BF16))
                    p_o = a2.enter_context(nc.psum_tensor("p_o", [128, 1024], F32))
                    p_n = a2.enter_context(nc.psum_tensor("p_n", [128, 8], F32))
                    p_2 = a2.enter_context(nc.psum_tensor("p_2", [128, 512], F32))
                    sb_w = a2sb("sb_w", 128, NT * 128, BF16)
                    sb_wT = a2sb("sb_wT", 128, NT * 128, BF16)
                    sb_oT = a2sb("sb_oT", 128, ND * 128, BF16)
                    sb_ob = a2sb("sb_ob", 128, D, BF16)
                    sb_xr = a2sb("sb_xr", 128, 2 * D, F32)
                    sb_pre = a2sb("sb_pre", 128, D, F32)
                    sb_sq = a2sb("sb_sq", 128, D, F32)
                    woB = a2sb("woB", 128, ND * D, BF16)
                    stag2 = a2sb("stag2", 128, 2 * D, F32)

                    wtokO = _stream_weight(nc, cx, woT_d, stag2, woB, 0, D, 0, D,
                                           nc.vector, VQ,
                                           prev_war=[cx.tok(TQ)], chan="ow")
                    cx.wait(nc.tensor, wtokO)
                    cx.wait(nc.tensor, qkv_done_v)
                    cx.wait(nc.tensor, qkv_done_s)

                    xr_tok = [None] * NT
                    cx.wait(nc.gpsimd, cx.tok(TQ))
                    for i in range(2):
                        xr_tok[i] = cx.dinc(nc.gpsimd.dma_start(
                            out=sb_xr[:, (i % 2) * D:(i % 2 + 1) * D],
                            in_=x_d[i * 128:(i + 1) * 128, :]), "xr%d" % (i % 2))

                    for i in range(NT):
                        for kc in range(4):
                            pm = p_s if kc % 2 == 0 else p_s2
                            if ("sev", kc % 2) in g:
                                cx.wait(nc.tensor, g[("sev", kc % 2)])
                            if i > 0 and kc == 0:
                                cx.wait(nc.scalar, g[("wtr",)])
                                cx.wait(nc.vector, g[("wtr",)])
                            mt = None
                            for d in range(ND):
                                mt = nc.tensor.matmul(
                                    pm[:],
                                    sb_QT[:, d * S + i * 128:d * S + (i + 1) * 128],
                                    sb_KT[:, d * S + kc * 512:d * S + (kc + 1) * 512],
                                    start=(d == 0), stop=(d == ND - 1))
                            mtok = cx.inc(mt, TQ)
                            for j in range(4):
                                kt = kc * 4 + j
                                cx.wait(nc.vector, mtok)
                                vw(nc.vector.reduce_max(
                                    st_[:, 0:1], pm[:, j * 128:(j + 1) * 128],
                                    axis=AX.X, negate=True))
                                vt0 = cx.inc(nc.vector.tensor_scalar_mul(
                                    st_[:, 1:2], st_[:, 0:1], 1.0 / 32.0), VQ)
                                cx.wait(nc.scalar, vt0)
                                stk = cx.inc(nc.scalar.activation(
                                    sb_w[:, kt * 128:(kt + 1) * 128],
                                    pm[:, j * 128:(j + 1) * 128], AF.Exp,
                                    bias=st_[:, 1:2], scale=1.0 / 32.0), SQ)
                                if j == 3:
                                    g[("sev", kc % 2)] = stk
                                cx.wait(nc.vector, stk)
                                vw(nc.vector.reduce_sum(
                                    st_[:, 2:3],
                                    sb_w[:, kt * 128:(kt + 1) * 128], axis=AX.X))
                                vw(nc.vector.reciprocal(st_[:, 3:4], st_[:, 2:3]))
                                cx.inc(nc.vector.tensor_scalar_mul(
                                    sb_w[:, kt * 128:(kt + 1) * 128],
                                    sb_w[:, kt * 128:(kt + 1) * 128],
                                    st_[:, 3:4]), VQ)
                        wnorm_done = cx.tok(VQ)
                        if debug and i == 0:
                            cx.wait(nc.gpsimd, wnorm_done)
                            dtk = cx.dinc(nc.gpsimd.dma_start(out=o_w0, in_=sb_w[:]), "dbg")
                            cx.wait(nc.tensor, dtk)
                            cx.wait(nc.vector, dtk)
                            cx.wait(nc.scalar, dtk)
                        for gi in range(4):
                            pt = p_tr if gi % 2 == 0 else p_tr2
                            if ("trc", gi % 2) in g:
                                cx.wait(nc.tensor, g[("trc", gi % 2)])
                            cx.wait(nc.tensor, wnorm_done)
                            if i > 0 and gi == 0:
                                cx.wait(nc.tensor, g[("pvmm",)])
                            mt = None
                            for j in range(4):
                                kt = gi * 4 + j
                                mt = nc.tensor.transpose(
                                    pt[:, j * 128:(j + 1) * 128],
                                    sb_w[:, kt * 128:(kt + 1) * 128], c_identb[:])
                            mtok = cx.inc(mt, TQ)
                            if gi == 3:
                                g[("wtr",)] = mtok
                            cx.wait(nc.scalar, mtok)
                            stk = nc.scalar.copy(sb_wT[:, gi * 512:(gi + 1) * 512],
                                                 pt[:])
                            g[("trc", gi % 2)] = cx.inc(stk, SQ)
                        wT_done = cx.tok(SQ)
                        cx.wait(nc.tensor, wT_done)
                        if ("pvev",) in g:
                            cx.wait(nc.tensor, g[("pvev",)])
                        mt = None
                        for kt in range(NT):
                            nc.tensor.matmul(
                                p_o[:, 0:512],
                                sb_wT[:, kt * 128:(kt + 1) * 128],
                                sb_V[:, kt * D:kt * D + 512],
                                start=(kt == 0), stop=(kt == NT - 1))
                            nc.tensor.matmul(
                                p_o[:, 512:1024],
                                sb_wT[:, kt * 128:(kt + 1) * 128],
                                sb_V[:, kt * D + 512:(kt + 1) * D],
                                start=(kt == 0), stop=(kt == NT - 1))
                            mt = nc.tensor.matmul(
                                p_n[:, 0:1],
                                sb_wT[:, kt * 128:(kt + 1) * 128],
                                c_onesb[:],
                                start=(kt == 0), stop=(kt == NT - 1))
                        mtok = cx.inc(mt, TQ)
                        g[("pvmm",)] = mtok
                        cx.wait(nc.vector, mtok)
                        if i > 0:
                            cx.wait(nc.vector, g[("obr",)])
                        vw(nc.vector.tensor_scalar_add(st_[:, 4:5], p_n[:, 0:1],
                                                       1.0e-6))
                        vw(nc.vector.reciprocal(st_[:, 5:6], st_[:, 4:5]))
                        vt = cx.inc(nc.vector.tensor_scalar_mul(
                            sb_ob[:], p_o[:], st_[:, 5:6]), VQ)
                        g[("pvev",)] = vt
                        if debug and i == 0:
                            cx.wait(nc.gpsimd, vt)
                            dtk = cx.dinc(nc.gpsimd.dma_start(out=o_ob0, in_=sb_ob[:]), "dbg")
                            cx.wait(nc.tensor, dtk)
                            cx.wait(nc.vector, dtk)
                            cx.wait(nc.scalar, dtk)
                        for gi in range(2):
                            pt = p_tr if gi % 2 == 0 else p_tr2
                            cx.wait(nc.tensor, g[("trc", gi % 2)])
                            cx.wait(nc.tensor, vt)
                            if i > 0 and gi == 0:
                                cx.wait(nc.tensor, g[("o2mm",)])
                            mt = None
                            for j in range(4):
                                d = gi * 4 + j
                                mt = nc.tensor.transpose(
                                    pt[:, j * 128:(j + 1) * 128],
                                    sb_ob[:, d * 128:(d + 1) * 128], c_identb[:])
                            mtok = cx.inc(mt, TQ)
                            if gi == 1:
                                g[("obr",)] = mtok
                            cx.wait(nc.scalar, mtok)
                            stk = nc.scalar.copy(sb_oT[:, gi * 512:(gi + 1) * 512],
                                                 pt[:])
                            g[("trc", gi % 2)] = cx.inc(stk, SQ)
                        oT_done = cx.tok(SQ)
                        cx.wait(nc.tensor, oT_done)
                        for ch in range(2):
                            if ("o2ev",) in g:
                                cx.wait(nc.tensor, g[("o2ev",)])
                            mt = None
                            for d in range(ND):
                                mt = nc.tensor.matmul(
                                    p_2[:],
                                    sb_oT[:, d * 128:(d + 1) * 128],
                                    woB[:, d * D + ch * 512:d * D + (ch + 1) * 512],
                                    start=(d == 0), stop=(d == ND - 1))
                            mtok = cx.inc(mt, TQ)
                            if ch == 1:
                                g[("o2mm",)] = mtok
                            cx.wait(nc.vector, mtok)
                            vt = cx.inc(nc.vector.tensor_tensor(
                                sb_pre[:, ch * 512:(ch + 1) * 512], p_2[:],
                                c_bobc[:, ch * 512:(ch + 1) * 512], ALU.add), VQ)
                            g[("o2ev",)] = vt
                        cx.wait(nc.vector, xr_tok[i])
                        vt = cx.inc(nc.vector.tensor_tensor(
                            sb_pre[:], sb_pre[:],
                            sb_xr[:, (i % 2) * D:(i % 2 + 1) * D], ALU.add), VQ)
                        if i + 2 < NT:
                            cx.wait(nc.gpsimd, vt)
                            xr_tok[i + 2] = cx.dinc(nc.gpsimd.dma_start(
                                out=sb_xr[:, ((i + 2) % 2) * D:((i + 2) % 2 + 1) * D],
                                in_=x_d[(i + 2) * 128:(i + 3) * 128, :]), "xr%d" % (i % 2))
                        if debug and i == 0:
                            dtk0 = cx.inc(nc.vector.tensor_copy(sb_sq[:], sb_pre[:]), VQ)
                            cx.wait(nc.gpsimd, dtk0)
                            dtk = cx.dinc(nc.gpsimd.dma_start(out=o_pre0, in_=sb_sq[:]), "dbg")
                            cx.wait(nc.vector, dtk)
                        vw(nc.vector.reduce_sum(st_[:, 6:7], sb_pre[:], axis=AX.X))
                        vw(nc.vector.tensor_scalar_mul(st_[:, 7:8], st_[:, 6:7],
                                                       1.0 / D))
                        vw(nc.vector.tensor_scalar(sb_pre[:], sb_pre[:],
                                                   st_[:, 7:8], None, ALU.subtract))
                        vt = cx.inc(nc.vector.scalar_tensor_tensor(
                            sb_sq[:], sb_pre[:], 1.0, sb_pre[:], ALU.mult, ALU.mult,
                            accum_out=st_[:, 8:9]), VQ)
                        cx.wait(nc.scalar, vt)
                        stk = cx.inc(nc.scalar.activation(
                            st_[:, 9:10], st_[:, 8:9], AF.Sqrt,
                            bias=c_eps5[:, 0:1], scale=1.0 / D), SQ)
                        cx.wait(nc.vector, stk)
                        vw(nc.vector.reciprocal(st_[:, 10:11], st_[:, 9:10]))
                        cx.inc(nc.vector.tensor_scalar_mul(
                            sb_x1[:, i * D:(i + 1) * D], sb_pre[:],
                            st_[:, 10:11]), VQ)
                    a2_done_v = cx.tok(VQ)
                    a2_done_s = cx.tok(SQ)
                    a2_done_t = cx.tok(TQ)
                    if debug:
                        cx.wait(nc.gpsimd, a2_done_v)
                        store_toks.append(cx.dinc(nc.gpsimd.dma_start(out=o_x1, in_=sb_x1[:]), "dbg"))

            with ExitStack() as bes:
                def bsb(name, p, c, dt=F32):
                    return bes.enter_context(nc.sbuf_tensor(name, [p, c], dt))

                sb_xnT = bsb("sb_xnT", 128, ND * S, BF16)
                sb_kT2 = bsb("sb_kT2", 128, 2 * S, BF16)
                sb_v2 = bsb("sb_v2", 128, NT * 256, BF16)
                sb_qT2h = bsb("sb_qT2h", 128, ND * 1024, BF16)
                fw_s = bsb("fw_s", 128, ND * 256, BF16)
                fw_q = bsb("fw_q", 128, ND * D, BF16)
                fw_o = bsb("fw_o", 128, ND * D, BF16)
                stg = bsb("stg", 128, 2 * D, F32)
                sb_f0 = bsb("sb_f0", 128, D, F32)
                sb_f1 = bsb("sb_f1", 128, D, F32)
                sb_f2 = bsb("sb_f2", 128, D, F32)
                sb_nb = bsb("sb_nb", 128, D, BF16)
                sb_pt = bsb("sb_pt", 128, S, BF16)
                sb_ptT = bsb("sb_ptT", 128, NT * 128, BF16)
                sb_oT2 = bsb("sb_oT2", 128, ND * 128, BF16)
                sb_h2fT = bsb("sb_h2fT", 128, ND * 128, F32)
                sb_srng = bsb("sb_srng", 128, 2 * D, BF16)
                sb_h2rng = bsb("sb_h2rng", 128, 2 * D, BF16)
                c_cos = bsb("c_cos", 128, S, BF16)
                c_sin = bsb("c_sin", 128, S, BF16)
                cx.wait(nc.sync, a2_done_t)
                cx.wait(nc.sync, a2_done_s)
                cx.wait(nc.sync, a2_done_v)
                cx.dinc(nc.sync.dma_start(out=c_cos[:], in_=cosb_d), "cld2")
                cs_tok = cx.dinc(nc.sync.dma_start(out=c_sin[:], in_=sinb_d), "cld2")
                cx.wait(nc.vector, cs_tok)

                with ExitStack() as b0:
                    pb_tr = b0.enter_context(nc.psum_tensor("pb_tr", [128, 512], BF16))
                    pb_tr2 = b0.enter_context(nc.psum_tensor("pb_tr2", [128, 512], BF16))
                    pb_mm = b0.enter_context(nc.psum_tensor("pb_mm", [128, 512], F32))
                    pb_mm2 = b0.enter_context(nc.psum_tensor("pb_mm2", [128, 512], F32))

                    cx.wait(nc.vector, a2_done_v)
                    cx.wait(nc.tensor, a2_done_s)
                    cx.wait(nc.tensor, a2_done_v)
                    for tt in range(NT):
                        x1t = sb_x1[:, tt * D:(tt + 1) * D]
                        if tt > 0:
                            cx.wait(nc.vector, g[("xntr", 1)])
                        vt = cx.inc(nc.vector.scalar_tensor_tensor(
                            sb_f0[:], x1t, 1.0, x1t, ALU.mult, ALU.mult,
                            accum_out=st_[:, 0:1]), VQ)
                        cx.wait(nc.scalar, vt)
                        stk = cx.inc(nc.scalar.activation(
                            st_[:, 1:2], st_[:, 0:1], AF.Sqrt,
                            bias=c_eps6[:, 0:1], scale=1.0 / D), SQ)
                        cx.wait(nc.vector, stk)
                        vw(nc.vector.reciprocal(st_[:, 2:3], st_[:, 1:2]))
                        vw(nc.vector.tensor_scalar_mul(sb_f1[:], x1t, st_[:, 2:3]))
                        vt = cx.inc(nc.vector.tensor_tensor(
                            sb_nb[:], sb_f1[:], c_g1bc[:], ALU.mult), VQ)
                        for gi in range(2):
                            pt = pb_tr if gi % 2 == 0 else pb_tr2
                            if ("xntrc", gi % 2) in g:
                                cx.wait(nc.tensor, g[("xntrc", gi % 2)])
                            cx.wait(nc.tensor, vt)
                            mt = None
                            for j in range(4):
                                d = gi * 4 + j
                                mt = nc.tensor.transpose(
                                    pt[:, j * 128:(j + 1) * 128],
                                    sb_nb[:, d * 128:(d + 1) * 128], c_identb[:])
                            mtok = cx.inc(mt, TQ)
                            g[("xntr", gi)] = mtok
                            cx.wait(nc.scalar, mtok)
                            sctk = None
                            for j in range(4):
                                d = gi * 4 + j
                                sctk = nc.scalar.copy(
                                    sb_xnT[:, d * S + tt * 128:d * S + (tt + 1) * 128],
                                    pt[:, j * 128:(j + 1) * 128])
                            g[("xntrc", gi % 2)] = cx.inc(sctk, SQ)
                    xnT_done = cx.tok(SQ)

                    wtok = _stream_weight(nc, cx, fwk_d, stg, fw_s, 0, D, 0, 256,
                                          nc.vector, VQ,
                                          prev_war=[a2_done_t, a2_done_s], chan="bw")
                    cx.wait(nc.tensor, wtok)
                    cx.wait(nc.tensor, xnT_done)
                    for m2 in range(2):
                        for nch in range(4):
                            kk = (m2 * 4 + nch) % 2
                            pm = pb_mm if kk == 0 else pb_mm2
                            if ("k2e", kk) in g:
                                cx.wait(nc.tensor, g[("k2e", kk)])
                            mt = None
                            for d in range(ND):
                                mt = nc.tensor.matmul(
                                    pm[:],
                                    fw_s[:, d * 256 + m2 * 128:d * 256 + (m2 + 1) * 128],
                                    sb_xnT[:, d * S + nch * 512:d * S + (nch + 1) * 512],
                                    start=(d == 0), stop=(d == ND - 1))
                            mtok = cx.inc(mt, TQ)
                            cx.wait(nc.vector, mtok)
                            c0 = nch * 512
                            vw(nc.vector.tensor_copy(sb_f1[:, 0:512], pm[:]))
                            nc.vector.tensor_copy(sb_f2[0:32, 0:512], sb_f1[32:64, 0:512])
                            nc.vector.tensor_copy(sb_f2[32:64, 0:512], sb_f1[0:32, 0:512])
                            nc.vector.tensor_copy(sb_f2[64:96, 0:512], sb_f1[96:128, 0:512])
                            nc.vector.tensor_copy(sb_f2[96:128, 0:512], sb_f1[64:96, 0:512])
                            nc.vector.tensor_tensor(sb_f1[:, 0:512], sb_f1[:, 0:512],
                                                    c_cos[:, c0:c0 + 512], ALU.mult)
                            nc.vector.tensor_tensor(sb_f2[:, 0:512], sb_f2[:, 0:512],
                                                    c_sin[:, c0:c0 + 512], ALU.mult)
                            vt = cx.inc(nc.vector.tensor_tensor(
                                sb_kT2[:, m2 * S + c0:m2 * S + c0 + 512],
                                sb_f1[:, 0:512], sb_f2[:, 0:512], ALU.add), VQ)
                            g[("k2e", kk)] = vt
                    k2_done = cx.tok(VQ)
                    wtokv = _stream_weight(nc, cx, fwv_d, stg, fw_s, 0, D, 0, 256,
                                           nc.vector, VQ,
                                           prev_war=[cx.tok(TQ), wtok], chan="bw")
                    cx.wait(nc.tensor, wtokv)
                    for tt in range(NT):
                        kk = tt % 2
                        pm = pb_mm if kk == 0 else pb_mm2
                        if ("v2e", kk) in g:
                            cx.wait(nc.tensor, g[("v2e", kk)])
                        mt = None
                        for d in range(ND):
                            mt = nc.tensor.matmul(
                                pm[:, 0:256],
                                sb_xnT[:, d * S + tt * 128:d * S + (tt + 1) * 128],
                                fw_s[:, d * 256:(d + 1) * 256],
                                start=(d == 0), stop=(d == ND - 1))
                        mtok = cx.inc(mt, TQ)
                        cx.wait(nc.vector, mtok)
                        vt = cx.inc(nc.vector.tensor_copy(
                            sb_v2[:, tt * 256:(tt + 1) * 256], pm[:, 0:256]), VQ)
                        g[("v2e", kk)] = vt
                    v2_done = cx.tok(VQ)
                    if debug:
                        cx.wait(nc.gpsimd, xnT_done)
                        cx.wait(nc.gpsimd, k2_done)
                        cx.wait(nc.gpsimd, v2_done)
                        store_toks.append(cx.dinc(nc.gpsimd.dma_start(out=o_xnT, in_=sb_xnT[:]), "dbg"))
                        store_toks.append(cx.dinc(nc.gpsimd.dma_start(out=o_kT2, in_=sb_kT2[:]), "dbg"))
                        store_toks.append(cx.dinc(nc.gpsimd.dma_start(out=o_v2, in_=sb_v2[:]), "dbg"))
                    wtokq = _stream_weight(nc, cx, fwq_d, stg, fw_q, 0, D, 0, D,
                                           nc.vector, VQ, prev_war=[wtokv], chan="bw")
                    wtoko = _stream_weight(nc, cx, fwo_d, stg, fw_o, 0, D, 0, D,
                                           nc.vector, VQ, prev_war=[wtokq], chan="bw")
                    b1_done_t = cx.tok(TQ)

                with ExitStack() as b2:
                    preg = [b2.enter_context(nc.psum_tensor(f"preg{r}", [128, 1024], F32))
                            for r in range(3)]
                    ptrb = b2.enter_context(nc.psum_tensor("ptrb", [128, 512], BF16))
                    pvo = b2.enter_context(nc.psum_tensor("pvo", [64, 128], F32))
                    runit = [0]

                    def next_region(eng):
                        r = runit[0] % 3
                        runit[0] += 1
                        if ("reg", r) in g:
                            cx.wait(eng, g[("reg", r)])
                        return r

                    cx.wait(nc.tensor, wtokq)
                    cx.wait(nc.tensor, wtoko)
                    cx.wait(nc.tensor, k2_done)
                    cx.wait(nc.tensor, v2_done)

                    half_done_t = None
                    for hf in range(2):
                        if half_done_t is not None:
                            cx.wait(nc.vector, half_done_t)
                        for m in range(ND):
                            for nch in range(2):
                                r = next_region(nc.tensor)
                                pm = preg[r]
                                c0 = hf * 1024 + nch * 512
                                mt = None
                                for d in range(ND):
                                    mt = nc.tensor.matmul(
                                        pm[:, 0:512],
                                        fw_q[:, d * D + m * 128:d * D + (m + 1) * 128],
                                        sb_xnT[:, d * S + c0:d * S + c0 + 512],
                                        start=(d == 0), stop=(d == ND - 1))
                                mtok = cx.inc(mt, TQ)
                                cx.wait(nc.vector, mtok)
                                if ("qrot",) in g:
                                    cx.wait(nc.vector, g[("qrot",)])
                                vt = cx.inc(nc.vector.tensor_copy(
                                    sb_f1[:, 0:512], pm[:, 0:512]), VQ)
                                cx.wait(nc.vector, vt)
                                g[("reg", r)] = vt
                                nc.vector.tensor_copy(sb_f2[0:32, 0:512],
                                                      sb_f1[32:64, 0:512])
                                nc.vector.tensor_copy(sb_f2[32:64, 0:512],
                                                      sb_f1[0:32, 0:512])
                                nc.vector.tensor_copy(sb_f2[64:96, 0:512],
                                                      sb_f1[96:128, 0:512])
                                nc.vector.tensor_copy(sb_f2[96:128, 0:512],
                                                      sb_f1[64:96, 0:512])
                                nc.vector.tensor_tensor(
                                    sb_f1[:, 0:512], sb_f1[:, 0:512],
                                    c_cos[:, c0:c0 + 512], ALU.mult)
                                nc.vector.tensor_tensor(
                                    sb_f2[:, 0:512], sb_f2[:, 0:512],
                                    c_sin[:, c0:c0 + 512], ALU.mult)
                                vt = cx.inc(nc.vector.tensor_tensor(
                                    sb_qT2h[:, m * 1024 + nch * 512:m * 1024 + (nch + 1) * 512],
                                    sb_f1[:, 0:512], sb_f2[:, 0:512], ALU.add), VQ)
                                g[("qrot",)] = vt
                        qh_done = cx.tok(VQ)
                        cx.wait(nc.tensor, qh_done)

                        for il in range(NT // 2):
                            qt = hf * (NT // 2) + il
                            klen = (qt + 1) * 128
                            nchk = (klen + 1023) // 1024
                            x1t = sb_x1[:, qt * D:(qt + 1) * D]
                            for h in range(HQ):
                                kv = h % HKV
                                ft = h // 2
                                rowlo = (h % 2) * 64
                                qslice = sb_qT2h[rowlo:rowlo + 64,
                                                 ft * 1024 + il * 128:ft * 1024 + (il + 1) * 128]
                                if ("pttr",) in g:
                                    cx.wait(nc.scalar, g[("pttr",)])
                                regs_used = []
                                for c in range(nchk):
                                    lc = min(1024, klen - c * 1024)
                                    r = next_region(nc.tensor)
                                    regs_used.append((r, lc))
                                    pm = preg[r]
                                    mt = None
                                    for sl in range((lc + 511) // 512):
                                        w = min(512, lc - sl * 512)
                                        koff = c * 1024 + sl * 512
                                        m2 = kv // 2
                                        rlo = (kv % 2) * 64
                                        mt = nc.tensor.matmul(
                                            pm[:, sl * 512:sl * 512 + w],
                                            qslice,
                                            sb_kT2[rlo:rlo + 64,
                                                   m2 * S + koff:m2 * S + koff + w],
                                            start=True, stop=True)
                                    mtok = cx.inc(mt, TQ)
                                    cx.wait(nc.vector, mtok)
                                    if (qt * 128) // 1024 == c:
                                        off = qt * 128 - c * 1024
                                        vw(nc.vector.tensor_tensor(
                                            pm[:, off:off + 128], pm[:, off:off + 128],
                                            c_cmask[:], ALU.add))
                                    vw(nc.vector.reduce_max(
                                        st_[:, 12 + c:13 + c], pm[:, 0:lc],
                                        axis=AX.X, negate=True))
                                    vt0 = cx.inc(nc.vector.tensor_scalar_mul(
                                        st_[:, 16 + c:17 + c], st_[:, 12 + c:13 + c],
                                        1.0 / 8.0), VQ)
                                    cx.wait(nc.scalar, vt0)
                                    stk = cx.inc(nc.scalar.activation(
                                        sb_pt[:, c * 1024:c * 1024 + lc],
                                        pm[:, 0:lc], AF.Exp,
                                        bias=st_[:, 16 + c:17 + c],
                                        scale=1.0 / 8.0), SQ)
                                    g[("reg", r)] = stk
                                    cx.wait(nc.vector, stk)
                                    vw(nc.vector.reduce_sum(
                                        st_[:, 20 + c:21 + c],
                                        sb_pt[:, c * 1024:c * 1024 + lc],
                                        axis=AX.X))
                                cx.wait(nc.vector, cx.tok(SQ))
                                if nchk == 1:
                                    vw(nc.vector.reciprocal(st_[:, 25:26],
                                                            st_[:, 20:21]))
                                    vsc = cx.inc(nc.vector.tensor_scalar_mul(
                                        sb_pt[:, 0:klen], sb_pt[:, 0:klen],
                                        st_[:, 25:26]), VQ)
                                else:
                                    vw(nc.vector.tensor_tensor(
                                        st_[:, 28:29], st_[:, 12:13], st_[:, 13:14],
                                        ALU.min))
                                    for c in range(2):
                                        vw(nc.vector.tensor_tensor(
                                            st_[:, 29:30], st_[:, 28:29],
                                            st_[:, 12 + c:13 + c], ALU.subtract))
                                        vt0 = cx.inc(nc.vector.tensor_scalar_mul(
                                            st_[:, 29 + 10 * c:30 + 10 * c],
                                            st_[:, 29:30], 1.0 / 8.0), VQ)
                                        cx.wait(nc.vector, vt0)
                                        cx.wait(nc.scalar, vt0)
                                        stk = cx.inc(nc.scalar.activation(
                                            st_[:, 30 + c:31 + c],
                                            st_[:, 29 + 10 * c:30 + 10 * c],
                                            AF.Exp), SQ)
                                        cx.wait(nc.vector, stk)
                                    vw(nc.vector.tensor_tensor(
                                        st_[:, 32:33], st_[:, 30:31], st_[:, 20:21],
                                        ALU.mult))
                                    vw(nc.vector.scalar_tensor_tensor(
                                        st_[:, 33:34], st_[:, 21:22], 1.0,
                                        st_[:, 31:32], ALU.mult, ALU.mult))
                                    vw(nc.vector.tensor_tensor(
                                        st_[:, 24:25], st_[:, 32:33], st_[:, 33:34],
                                        ALU.add))
                                    vw(nc.vector.reciprocal(st_[:, 25:26],
                                                            st_[:, 24:25]))
                                    vw(nc.vector.tensor_tensor(
                                        st_[:, 34:35], st_[:, 30:31], st_[:, 25:26],
                                        ALU.mult))
                                    vw(nc.vector.tensor_tensor(
                                        st_[:, 35:36], st_[:, 31:32], st_[:, 25:26],
                                        ALU.mult))
                                    nc.vector.tensor_scalar_mul(
                                        sb_pt[:, 0:1024], sb_pt[:, 0:1024],
                                        st_[:, 34:35])
                                    vsc = cx.inc(nc.vector.tensor_scalar_mul(
                                        sb_pt[:, 1024:klen], sb_pt[:, 1024:klen],
                                        st_[:, 35:36]), VQ)
                                nkt = qt + 1
                                cx.wait(nc.tensor, vsc)
                                if ("ptTw",) in g:
                                    cx.wait(nc.tensor, g[("ptTw",)])
                                for gi in range((nkt + 3) // 4):
                                    if ("btrc",) in g:
                                        cx.wait(nc.tensor, g[("btrc",)])
                                    mt = None
                                    nj = min(4, nkt - gi * 4)
                                    for j in range(nj):
                                        kt = gi * 4 + j
                                        mt = nc.tensor.transpose(
                                            ptrb[:, j * 128:(j + 1) * 128],
                                            sb_pt[:, kt * 128:(kt + 1) * 128],
                                            c_identb[:])
                                    mtok = cx.inc(mt, TQ)
                                    g[("pttr",)] = mtok
                                    cx.wait(nc.scalar, mtok)
                                    stk = nc.scalar.copy(
                                        sb_ptT[:, gi * 512:gi * 512 + nj * 128],
                                        ptrb[:, 0:nj * 128])
                                    g[("btrc",)] = cx.inc(stk, SQ)
                                ptT_done = cx.tok(SQ)
                                cx.wait(nc.tensor, ptT_done)
                                if ("pvoe",) in g:
                                    cx.wait(nc.tensor, g[("pvoe",)])
                                mt = None
                                for kt in range(nkt):
                                    mt = nc.tensor.matmul(
                                        pvo[:],
                                        sb_v2[:, kt * 256 + kv * 64:kt * 256 + (kv + 1) * 64],
                                        sb_ptT[:, kt * 128:(kt + 1) * 128],
                                        start=(kt == 0), stop=(kt == nkt - 1))
                                mtok = cx.inc(mt, TQ)
                                g[("ptTw",)] = mtok
                                cx.wait(nc.vector, mtok)
                                if ("oT2w",) in g and h == 0:
                                    cx.wait(nc.vector, g[("oT2w",)])
                                vt = cx.inc(nc.vector.tensor_copy(
                                    sb_oT2[rowlo:rowlo + 64, ft * 128:(ft + 1) * 128],
                                    pvo[0:64, :]), VQ)
                                g[("pvoe",)] = vt
                            heads_done = cx.tok(VQ)
                            cx.wait(nc.tensor, heads_done)
                            for ch in range(2):
                                r = next_region(nc.tensor)
                                pm = preg[r]
                                mt = None
                                for d in range(ND):
                                    mt = nc.tensor.matmul(
                                        pm[:, 0:512],
                                        sb_oT2[:, d * 128:(d + 1) * 128],
                                        fw_o[:, d * D + ch * 512:d * D + (ch + 1) * 512],
                                        start=(d == 0), stop=(d == ND - 1))
                                mtok = cx.inc(mt, TQ)
                                if ch == 1:
                                    g[("oT2w",)] = mtok
                                cx.wait(nc.vector, mtok)
                                vt = cx.inc(nc.vector.tensor_tensor(
                                    sb_f1[:, ch * 512:(ch + 1) * 512], pm[:, 0:512],
                                    x1t[:, ch * 512:(ch + 1) * 512], ALU.add), VQ)
                                g[("reg", r)] = vt
                            if ("sst", qt % 2) in g:
                                cx.wait(nc.vector, g[("sst", qt % 2)])
                            vt = cx.inc(nc.vector.tensor_tensor(
                                sb_srng[:, (qt % 2) * D:(qt % 2 + 1) * D],
                                sb_f1[:], x1t, ALU.add), VQ)
                            cx.wait(nc.gpsimd, vt)
                            stok = cx.dinc(nc.gpsimd.dma_start(
                                out=o_s[qt * 128:(qt + 1) * 128, :],
                                in_=sb_srng[:, (qt % 2) * D:(qt % 2 + 1) * D]),
                                "ss%d" % (qt % 2))
                            g[("sst", qt % 2)] = stok
                            store_toks.append(stok)
                            if ("lgf0",) in g:
                                cx.wait(nc.vector, g[("lgf0",)])
                            vt = cx.inc(nc.vector.scalar_tensor_tensor(
                                sb_f0[:], sb_f1[:], 1.0, sb_f1[:], ALU.mult,
                                ALU.mult, accum_out=st_[:, 36:37]), VQ)
                            cx.wait(nc.scalar, vt)
                            stk = cx.inc(nc.scalar.activation(
                                st_[:, 37:38], st_[:, 36:37], AF.Sqrt,
                                bias=c_eps6[:, 0:1], scale=1.0 / D), SQ)
                            cx.wait(nc.vector, stk)
                            vw(nc.vector.reciprocal(st_[:, 38:39], st_[:, 37:38]))
                            vw(nc.vector.tensor_scalar_mul(sb_f0[:], sb_f1[:],
                                                           st_[:, 38:39]))
                            vt = cx.inc(nc.vector.tensor_tensor(
                                sb_f2[:], sb_f0[:], c_g2bc[:], ALU.mult), VQ)
                            for gi in range(2):
                                r = next_region(nc.tensor)
                                pm = preg[r]
                                cx.wait(nc.tensor, vt)
                                if gi == 0 and ("h2fTw",) in g:
                                    cx.wait(nc.tensor, g[("h2fTw",)])
                                mt = None
                                for j in range(4):
                                    d = gi * 4 + j
                                    mt = nc.tensor.transpose(
                                        pm[:, j * 128:(j + 1) * 128],
                                        sb_f2[:, d * 128:(d + 1) * 128],
                                        c_identf[:])
                                mtok = cx.inc(mt, TQ)
                                cx.wait(nc.vector, mtok)
                                vt2 = cx.inc(nc.vector.tensor_copy(
                                    sb_h2fT[:, gi * 512:(gi + 1) * 512],
                                    pm[:, 0:512]), VQ)
                                g[("reg", r)] = vt2
                            h2T_evac = cx.tok(VQ)
                            if ("h2st", qt % 2) in g:
                                cx.wait(nc.vector, g[("h2st", qt % 2)])
                            vt = cx.inc(nc.vector.tensor_copy(
                                sb_h2rng[:, (qt % 2) * D:(qt % 2 + 1) * D],
                                sb_h2fT[:]), VQ)
                            cx.wait(nc.scalar, vt)
                            h2stok = None
                            for d in range(ND):
                                h2stok = cx.dinc(nc.scalar.dma_start(
                                    out=o_h2T[d * 128:(d + 1) * 128,
                                              qt * 128:(qt + 1) * 128],
                                    in_=sb_h2rng[:, (qt % 2) * D + d * 128:
                                                 (qt % 2) * D + (d + 1) * 128]),
                                    "h2s%d" % (qt % 2))
                            g[("h2st", qt % 2)] = h2stok
                            store_toks.append(h2stok)
                            r = next_region(nc.tensor)
                            pm = preg[r]
                            cx.wait(nc.tensor, h2T_evac)
                            mt = None
                            for d in range(ND):
                                mt = nc.tensor.matmul(
                                    pm[0:8, 0:128],
                                    c_rw[:, d * E:(d + 1) * E],
                                    sb_h2fT[:, d * 128:(d + 1) * 128],
                                    start=(d == 0), stop=(d == ND - 1))
                            mtok = cx.inc(mt, TQ)
                            g[("h2fTw",)] = mtok
                            cx.wait(nc.scalar, mtok)
                            stk = cx.inc(nc.scalar.activation(
                                sb_f0[:, 0:128][0:8, :],
                                pm[0:8, 0:128], AF.Identity,
                                bias=c_rb[:, 0:1], scale=1.0), SQ)
                            g[("reg", r)] = stk
                            r2 = next_region(nc.tensor)
                            pm2 = preg[r2]
                            cx.wait(nc.tensor, stk)
                            mtok = cx.inc(nc.tensor.transpose(
                                pm2[:, 0:8], sb_f0[:, 0:128][0:8, :],
                                c_ident8[:]), TQ)
                            g[("lgf0",)] = mtok
                            cx.wait(nc.vector, mtok)
                            vt2 = cx.inc(nc.vector.tensor_copy(
                                sb_lgtok[:, qt * E:(qt + 1) * E],
                                pm2[:, 0:8]), VQ)
                            g[("reg", r2)] = vt2
                        if debug and hf == 0:
                            cx.wait(nc.gpsimd, qh_done)
                            store_toks.append(cx.dinc(nc.gpsimd.dma_start(out=o_qT2h, in_=sb_qT2h[:]), "dbg"))
                        half_done_t = cx.tok(TQ)
                    b2_done = cx.tok(SQ)

            with ExitStack() as res_:
                pr_a = res_.enter_context(nc.psum_tensor("pr_a", [128, 8], F32))
                pr_b = res_.enter_context(nc.psum_tensor("pr_b", [128, 8], F32))
                pr_c0 = res_.enter_context(nc.psum_tensor("pr_c0", [128, 8], F32))
                pr_c1 = res_.enter_context(nc.psum_tensor("pr_c1", [128, 8], F32))
                pr_t0 = res_.enter_context(nc.psum_tensor("pr_t0", [1, 8], F32))
                pr_t1 = res_.enter_context(nc.psum_tensor("pr_t1", [1, 8], F32))
                sb_t8 = sb("sb_t8", 128, 8, F32)
                sb_e0 = sb("sb_e0", 128, 8, F32)
                sb_e1 = sb("sb_e1", 128, 8, F32)
                sb_oh0 = sb("sb_oh0", 128, 8, F32)
                sb_oh1 = sb("sb_oh1", 128, 8, F32)
                sb_c0 = sb("sb_c0", 128, 8, F32)
                sb_c01 = sb("sb_c01", 128, 8, F32)

                cx.wait(nc.vector, b2_done)
                cx.wait(nc.vector, cx.tok(VQ) if False else cx.tok(VQ))
                for tt in range(NT):
                    lgv = sb_lgtok[:, tt * E:(tt + 1) * E]
                    vw(nc.vector.max(sb_t8[:], lgv))
                    vw(nc.vector.tensor_scalar(sb_e0[:], lgv, sb_t8[:, 0:1],
                                               None, ALU.is_ge))
                    vw(nc.vector.tensor_tensor(sb_e0[:], sb_e0[:], c_iom9[:],
                                               ALU.mult))
                    vw(nc.vector.tensor_scalar_add(sb_e0[:], sb_e0[:], 999.0))
                    vw(nc.vector.tensor_reduce(sb_idx0[:, tt:tt + 1], sb_e0[:],
                                               axis=AX.X, op=ALU.min))
                    vw(nc.vector.tensor_scalar(sb_e0[:], c_iota8[:],
                                               sb_idx0[:, tt:tt + 1], None,
                                               ALU.is_equal))
                    vw(nc.vector.scalar_tensor_tensor(sb_e1[:], sb_e0[:], NEG,
                                                      lgv, ALU.mult, ALU.add))
                    vw(nc.vector.reduce_max(st_[:, 40:41], sb_e1[:], axis=AX.X))
                    vw(nc.vector.tensor_scalar(sb_e1[:], sb_e1[:], st_[:, 40:41],
                                               None, ALU.is_ge))
                    vw(nc.vector.tensor_tensor(sb_e1[:], sb_e1[:], c_iom9[:],
                                               ALU.mult))
                    vw(nc.vector.tensor_scalar_add(sb_e1[:], sb_e1[:], 999.0))
                    vw(nc.vector.tensor_reduce(sb_idx1[:, tt:tt + 1], sb_e1[:],
                                               axis=AX.X, op=ALU.min))
                    vt = vw(nc.vector.tensor_tensor(st_[:, 41:42], st_[:, 40:41],
                                                    sb_t8[:, 0:1], ALU.subtract))
                    cx.wait(nc.scalar, vt)
                    stk = cx.inc(nc.scalar.activation(st_[:, 42:43], st_[:, 41:42],
                                                      AF.Exp), SQ)
                    cx.wait(nc.vector, stk)
                    vw(nc.vector.tensor_scalar_add(st_[:, 43:44], st_[:, 42:43],
                                                   1.0))
                    vw(nc.vector.reciprocal(st_[:, 44:45], st_[:, 43:44]))
                    vw(nc.vector.tensor_copy(sb_sc0[:, tt:tt + 1], st_[:, 44:45]))
                    vw(nc.vector.tensor_tensor(sb_sc1[:, tt:tt + 1], st_[:, 42:43],
                                               st_[:, 44:45], ALU.mult))
                    vw(nc.vector.tensor_scalar(sb_oh0[:], c_iota8[:],
                                               sb_idx0[:, tt:tt + 1], None,
                                               ALU.is_equal))
                    vt = vw(nc.vector.tensor_scalar(
                        sb_oh1[:], c_iota8[:], sb_idx1[:, tt:tt + 1], None,
                        ALU.is_equal))
                    cx.wait(nc.tensor, vt)
                    if tt > 0:
                        cx.wait(nc.tensor, g[("rcu",)])
                    nc.tensor.matmul(pr_c0[:], c_triu[:], sb_oh0[:],
                                     start=True, stop=True)
                    nc.tensor.matmul(pr_c1[:], c_triu[:], sb_oh1[:],
                                     start=True, stop=True)
                    nc.tensor.matmul(pr_t0[:], c_onesc[:], sb_oh0[:],
                                     start=True, stop=True)
                    nc.tensor.matmul(pr_t1[:], c_onesc[:], sb_oh1[:],
                                     start=True, stop=True)
                    nc.tensor.matmul(pr_a[:], c_ones1[:], sb_run0[:],
                                     start=True, stop=True)
                    mtok = cx.inc(nc.tensor.matmul(pr_b[:], c_ones1[:],
                                                   sb_run01[:],
                                                   start=True, stop=True), TQ)
                    cx.wait(nc.vector, mtok)
                    vw(nc.vector.tensor_copy(sb_c0[:], pr_c0[:]))
                    vw(nc.vector.tensor_tensor(sb_c01[:], sb_c0[:], pr_c1[:],
                                               ALU.add))
                    vw(nc.vector.scalar_tensor_tensor(sb_c0[:], sb_c0[:], 1.0,
                                                      pr_a[:], ALU.mult, ALU.add))
                    vw(nc.vector.scalar_tensor_tensor(
                        sb_e0[:], sb_c0[:], 1.0, sb_oh0[:], ALU.mult, ALU.mult,
                        accum_out=sb_cs0[:, tt:tt + 1]))
                    vw(nc.vector.scalar_tensor_tensor(sb_c01[:], sb_c01[:], 1.0,
                                                      pr_b[:], ALU.mult, ALU.add))
                    vw(nc.vector.scalar_tensor_tensor(
                        sb_e1[:], sb_c01[:], 1.0, sb_oh1[:], ALU.mult, ALU.mult,
                        accum_out=sb_cs01[:, tt:tt + 1]))
                    vw(nc.vector.tensor_tensor(sb_run0[:], sb_run0[:], pr_t0[:],
                                               ALU.add))
                    vw(nc.vector.tensor_tensor(sb_run1[:], sb_run1[:], pr_t1[:],
                                               ALU.add))
                    vt = vw(nc.vector.tensor_tensor(
                        sb_run01[:], sb_run0[:], sb_run1[:], ALU.add))
                    g[("rcu",)] = vt
                rt_done = cx.tok(VQ)
                cx.wait(nc.gpsimd, rt_done)
                for dst, srcb in [(o_idx0, sb_idx0), (o_idx1, sb_idx1),
                                  (o_sc0, sb_sc0), (o_sc1, sb_sc1),
                                  (o_cs0, sb_cs0), (o_cs01, sb_cs01)]:
                    store_toks.append(cx.dinc(nc.gpsimd.dma_start(
                        out=dst, in_=srcb[:]), "ro"))
                store_toks.append(cx.dinc(nc.gpsimd.dma_start(
                    out=o_cnt[0:1, :], in_=sb_run0[:]), "ro"))
                store_toks.append(cx.dinc(nc.gpsimd.dma_start(
                    out=o_cnt[1:2, :], in_=sb_run1[:]), "ro"))

                for ch in ("ss0", "ss1", "h2s0", "h2s1", "ro", "dbg"):
                    if ch in cx.cnt:
                        cx.wait(nc.sync, cx.tok(ch))

    return nc


def host_prepare(inputs):
    f32 = np.float32
    X = np.ascontiguousarray(np.asarray(inputs["L0_attn_batch"], f32))
    out = {}
    out["shared"] = sh = {}
    sh["wqT"] = np.ascontiguousarray(np.asarray(inputs["L0_attn_Wq"], f32).T)
    sh["wkT"] = np.ascontiguousarray(np.asarray(inputs["L0_attn_Wk"], f32).T)
    sh["wvT"] = np.ascontiguousarray(np.asarray(inputs["L0_attn_Wv"], f32).T)
    sh["woT"] = np.ascontiguousarray(np.asarray(inputs["L0_attn_Wo"], f32).T)
    sh["bq128"] = np.ascontiguousarray(
        np.asarray(inputs["L0_attn_bq"], f32).reshape(ND, 128).T)
    sh["bk128"] = np.ascontiguousarray(
        np.asarray(inputs["L0_attn_bk"], f32).reshape(ND, 128).T)
    sh["bv_bc"] = np.ascontiguousarray(
        np.broadcast_to(np.asarray(inputs["L0_attn_bv"], f32), (128, D)))
    sh["bo_bc"] = np.ascontiguousarray(
        np.broadcast_to(np.asarray(inputs["L0_attn_bo"], f32), (128, D)))
    sh["g1_bc"] = np.ascontiguousarray(
        np.broadcast_to(np.asarray(inputs["L0_ffn_g1"], f32), (128, D)))
    sh["g2_bc"] = np.ascontiguousarray(
        np.broadcast_to(np.asarray(inputs["L0_ffn_g2"], f32), (128, D)))
    sh["fwq"] = np.ascontiguousarray(np.asarray(inputs["L0_ffn_wq"], f32))
    sh["fwk"] = np.ascontiguousarray(np.asarray(inputs["L0_ffn_wk"], f32))
    sh["fwv"] = np.ascontiguousarray(np.asarray(inputs["L0_ffn_wv"], f32))
    sh["fwo"] = np.ascontiguousarray(np.asarray(inputs["L0_ffn_wo"], f32))
    sh["rw"] = np.ascontiguousarray(np.asarray(inputs["L0_ffn_router_w"], f32))
    sh["rb8"] = np.ascontiguousarray(
        np.asarray(inputs["L0_ffn_router_b"], f32).reshape(E, 1))
    thetas = 10000.0 ** (-np.arange(0, HD, 2, dtype=f32) / HD)
    ticks = np.outer(np.arange(S, dtype=f32), thetas)
    ticks2 = np.tile(ticks, (1, 2))
    cosf = np.cos(ticks2).astype(f32)
    sinf = np.sin(ticks2).astype(f32)
    cosb = np.empty((128, S), f32)
    sinb = np.empty((128, S), f32)
    for p in range(128):
        hd = p % HD
        cosb[p] = cosf[:, hd]
        sgn = -1.0 if hd < HD // 2 else 1.0
        sinb[p] = sgn * sinf[:, hd]
    sh["cosb"] = cosb.astype(BF)
    sh["sinb"] = sinb.astype(BF)
    sh["identb"] = np.eye(128).astype(BF)
    sh["identf"] = np.eye(128, dtype=f32)
    sh["ident8"] = np.eye(8, dtype=f32)
    cm = np.zeros((128, 128), f32)
    for p in range(128):
        cm[p, p + 1:] = NEG
    sh["cmask"] = cm
    sh["triu"] = np.triu(np.ones((128, 128), f32))
    sh["iota8"] = np.broadcast_to(np.arange(8, dtype=f32), (128, 8)).copy()
    sh["iota8m999"] = sh["iota8"] - 999.0
    sh["ones_bf"] = np.ones((128, 1)).astype(BF)
    sh["ones1"] = np.ones((1, 128), f32)
    sh["ones128c"] = np.ones((128, 1), f32)
    out["x"] = [np.ascontiguousarray(X[c]) for c in range(NCORES)]
    return out


def kernel(**inputs):
    hp = host_prepare(inputs)
    nc1 = build_launch1()
    in_maps = []
    for c in range(NCORES):
        m = dict(hp["shared"])
        m["x"] = hp["x"][c]
        in_maps.append(m)
    res1 = run_bass_kernel_spmd(nc1, in_maps, core_ids=list(range(NCORES)))
    raise NotImplementedError("launch 2 pending")


def build_launch2():
    nc = bass.Bass("TRN2", target_bir_lowering=False)

    table_d = nc.dram_tensor("table", [2 * NCORES, E], F32, kind="ExternalInput").ap()
    pmask_d = nc.dram_tensor("pmask", [2 * NCORES, 2], F32, kind="ExternalInput").ap()
    s_d = nc.dram_tensor("s_in", [S, D], BF16, kind="ExternalInput").ap()
    h2T_d = nc.dram_tensor("h2T_in", [D, S], BF16, kind="ExternalInput").ap()
    idx0_d = nc.dram_tensor("idx0", [128, NT], F32, kind="ExternalInput").ap()
    idx1_d = nc.dram_tensor("idx1", [128, NT], F32, kind="ExternalInput").ap()
    sc0_d = nc.dram_tensor("sc0", [128, NT], F32, kind="ExternalInput").ap()
    sc1_d = nc.dram_tensor("sc1", [128, NT], F32, kind="ExternalInput").ap()
    cs0_d = nc.dram_tensor("cs0", [128, NT], F32, kind="ExternalInput").ap()
    cs01_d = nc.dram_tensor("cs01", [128, NT], F32, kind="ExternalInput").ap()
    mw1_d = nc.dram_tensor("mw1", [E * D, HID], F32, kind="ExternalInput").ap()
    mw2_d = nc.dram_tensor("mw2", [E * D, HID], F32, kind="ExternalInput").ap()
    mw3_d = nc.dram_tensor("mw3", [E * HID, D], F32, kind="ExternalInput").ap()
    iota8_d = nc.dram_tensor("iota8", [128, E], F32, kind="ExternalInput").ap()
    identb_d = nc.dram_tensor("identb", [128, 128], BF16, kind="ExternalInput").ap()
    ones1_d = nc.dram_tensor("ones1", [1, 128], F32, kind="ExternalInput").ap()
    o_out = nc.dram_tensor("o_out", [S, D], F32, kind="ExternalOutput").ap()

    es = ExitStack()
    cx = Ctx(nc, es)

    def sb(name, p, c, dt=F32):
        return es.enter_context(nc.sbuf_tensor(name, [p, c], dt))

    c_iota8 = sb("c_iota8", 128, E, F32)
    c_identb = sb("c_identb", 128, 128, BF16)
    c_ones1 = sb("c_ones1", 1, 128, F32)
    c_eps5 = sb("c_eps5", 128, 1, F32)
    sb_tab = sb("sb_tab", 2 * NCORES, E, F32)
    sb_pm = sb("sb_pm", 2 * NCORES, 2, F32)
    sb_b0r = sb("sb_b0r", 1, E, F32)
    sb_b01r = sb("sb_b01r", 1, E, F32)
    sb_b0bc = sb("sb_b0bc", 128, E, F32)
    sb_b01bc = sb("sb_b01bc", 128, E, F32)
    sb_i0 = sb("sb_i0", 128, NT, F32)
    sb_i1 = sb("sb_i1", 128, NT, F32)
    sb_s0 = sb("sb_s0", 128, NT, F32)
    sb_s1 = sb("sb_s1", 128, NT, F32)
    sb_c0 = sb("sb_c0", 128, NT, F32)
    sb_c01 = sb("sb_c01", 128, NT, F32)
    sb_bt = sb("sb_bt", 128, NT, F32)
    sb_ae = sb("sb_ae", 128, E * NT, F32)
    sb_oh = sb("sb_oh", 128, E, F32)
    sb_sc = sb("sb_sc", 128, E, F32)
    st_ = sb("st2_", 128, 16, F32)
    sb_h2T = sb("sb_h2T", 128, ND * S, BF16)
    sb_acc = sb("sb_acc", 128, NT * D, F32)

    TQ2, VQ2, SQ2 = "tq", "vq", "sq"

    with es:
        with nc.Block():
            g = {}

            def vw(instr):
                tok = cx.inc(instr, VQ2)
                cx.wait(nc.vector, tok)
                return tok

            for dst, srcd in [(c_iota8, iota8_d), (c_identb, identb_d),
                              (c_ones1, ones1_d), (sb_tab, table_d),
                              (sb_pm, pmask_d), (sb_i0, idx0_d), (sb_i1, idx1_d),
                              (sb_s0, sc0_d), (sb_s1, sc1_d), (sb_c0, cs0_d),
                              (sb_c01, cs01_d)]:
                cx.dinc(nc.sync.dma_start(out=dst[:], in_=srcd), "cld")
            for d in range(ND):
                cx.dinc(nc.sync.dma_start(
                    out=sb_h2T[:, d * S:(d + 1) * S],
                    in_=h2T_d[d * 128:(d + 1) * 128, :]), "cld")
            consts_tok = cx.tok("cld")
            for eng in (nc.vector, nc.scalar, nc.tensor, nc.gpsimd):
                cx.wait(eng, consts_tok)
            nc.vector.memset(c_eps5[:], 1.0e-5)

            with ExitStack() as rsc:
                pb0 = rsc.enter_context(nc.psum_tensor("pb0", [1, E], F32))
                pb01 = rsc.enter_context(nc.psum_tensor("pb01", [1, E], F32))
                pbc = rsc.enter_context(nc.psum_tensor("pbc", [128, E], F32))
                pbc2 = rsc.enter_context(nc.psum_tensor("pbc2", [128, E], F32))
                nc.tensor.matmul(pb0[:], sb_pm[:, 0:1], sb_tab[:],
                                 start=True, stop=True)
                mtok = cx.inc(nc.tensor.matmul(pb01[:], sb_pm[:, 1:2], sb_tab[:],
                                               start=True, stop=True), TQ2)
                cx.wait(nc.vector, mtok)
                vw(nc.vector.tensor_copy(sb_b0r[:], pb0[:]))
                vt = vw(nc.vector.tensor_copy(sb_b01r[:], pb01[:]))
                cx.wait(nc.tensor, vt)
                nc.tensor.matmul(pbc[:], c_ones1[:], sb_b0r[:],
                                 start=True, stop=True)
                mtok = cx.inc(nc.tensor.matmul(pbc2[:], c_ones1[:], sb_b01r[:],
                                               start=True, stop=True), TQ2)
                cx.wait(nc.vector, mtok)
                vw(nc.vector.tensor_copy(sb_b0bc[:], pbc[:]))
                vw(nc.vector.tensor_copy(sb_b01bc[:], pbc2[:]))

                for tt in range(NT):
                    vw(nc.vector.tensor_scalar(sb_oh[:], c_iota8[:],
                                               sb_i0[:, tt:tt + 1], None,
                                               ALU.is_equal))
                    vw(nc.vector.scalar_tensor_tensor(
                        sb_sc[:], sb_oh[:], 1.0, sb_b0bc[:], ALU.mult, ALU.mult,
                        accum_out=st_[:, 0:1]))
                    vw(nc.vector.tensor_tensor(st_[:, 1:2], st_[:, 0:1],
                                               sb_c0[:, tt:tt + 1], ALU.add))
                    vw(nc.vector.tensor_scalar(st_[:, 2:3], st_[:, 1:2], CAP,
                                               None, ALU.is_lt))
                    vw(nc.vector.tensor_scalar(sb_oh[:], c_iota8[:],
                                               sb_i1[:, tt:tt + 1], None,
                                               ALU.is_equal))
                    vw(nc.vector.scalar_tensor_tensor(
                        sb_sc[:], sb_oh[:], 1.0, sb_b01bc[:], ALU.mult, ALU.mult,
                        accum_out=st_[:, 3:4]))
                    vw(nc.vector.tensor_tensor(st_[:, 4:5], st_[:, 3:4],
                                               sb_c01[:, tt:tt + 1], ALU.add))
                    vw(nc.vector.tensor_scalar(st_[:, 5:6], st_[:, 4:5], CAP,
                                               None, ALU.is_lt))
                    vw(nc.vector.tensor_tensor(st_[:, 6:7], sb_s0[:, tt:tt + 1],
                                               st_[:, 2:3], ALU.mult))
                    vw(nc.vector.scalar_tensor_tensor(
                        st_[:, 7:8], sb_s1[:, tt:tt + 1], 1.0, st_[:, 5:6],
                        ALU.mult, ALU.mult))
                    vw(nc.vector.tensor_tensor(st_[:, 8:9], st_[:, 6:7],
                                               st_[:, 7:8], ALU.add))
                    vw(nc.vector.tensor_tensor(st_[:, 9:10], sb_s0[:, tt:tt + 1],
                                               sb_s1[:, tt:tt + 1], ALU.add))
                    vw(nc.vector.tensor_tensor(sb_bt[:, tt:tt + 1], st_[:, 9:10],
                                               st_[:, 8:9], ALU.subtract))
                    vw(nc.vector.tensor_tensor(st_[:, 10:11], sb_i0[:, tt:tt + 1],
                                               st_[:, 2:3], ALU.mult))
                    vw(nc.vector.tensor_tensor(st_[:, 11:12], sb_i1[:, tt:tt + 1],
                                               st_[:, 5:6], ALU.mult))
                    vw(nc.vector.tensor_tensor(st_[:, 12:13], st_[:, 10:11],
                                               st_[:, 11:12], ALU.max))
                    for e in range(E):
                        vw(nc.vector.tensor_scalar(
                            sb_ae[:, e * NT + tt:e * NT + tt + 1],
                            st_[:, 12:13], float(e), st_[:, 8:9],
                            ALU.is_equal, ALU.mult))
                rt_done = cx.tok(VQ2)

            with ExitStack() as msc:
                pt12a = msc.enter_context(nc.psum_tensor("pt12a", [128, 512], F32))
                pt12b = msc.enter_context(nc.psum_tensor("pt12b", [128, 512], F32))
                py0 = msc.enter_context(nc.psum_tensor("py0", [128, 1024], F32))
                py1 = msc.enter_context(nc.psum_tensor("py1", [128, 1024], F32))
                fw1 = msc.enter_context(nc.sbuf_tensor("fw1", [128, 2 * ND * 1024], BF16))
                fw2 = msc.enter_context(nc.sbuf_tensor("fw2", [128, 2 * ND * 1024], BF16))
                fw3 = msc.enter_context(nc.sbuf_tensor("fw3", [128, 2 * ND * 1024], BF16))
                wstag = msc.enter_context(nc.sbuf_tensor("wstag", [128, 2 * 1024], F32))
                sb_t1 = msc.enter_context(nc.sbuf_tensor("sb_t1", [128, 256], F32))
                sb_pr = msc.enter_context(nc.sbuf_tensor("sb_pr", [128, 256], F32))
                sb_hs = msc.enter_context(nc.sbuf_tensor("sb_hs", [128, 2 * 256], BF16))

                def load_whalf(e, hh, slot, war):
                    vtoks, stoks = [], []
                    for mat, dram, buf, ceng, ctoks in (
                            (1, mw1_d, fw1, nc.vector, vtoks),
                            (2, mw2_d, fw2, nc.vector, vtoks),
                            (3, mw3_d, fw3, nc.scalar, stoks)):
                        for d in range(ND):
                            k = len(vtoks) + len(stoks)
                            pp = k % 2
                            chain = (vtoks + stoks)
                            if k >= 2:
                                cx.wait(nc.sync, chain[k - 2])
                            elif war:
                                for w_ in war:
                                    cx.wait(nc.sync, w_)
                            if mat == 3:
                                srcap = dram[e * HID + hh * 1024 + d * 128:
                                             e * HID + hh * 1024 + (d + 1) * 128, :]
                            else:
                                srcap = dram[e * D + d * 128:e * D + (d + 1) * 128,
                                             hh * 1024:(hh + 1) * 1024]
                            t = cx.dinc(nc.sync.dma_start(
                                out=wstag[:, pp * 1024:(pp + 1) * 1024],
                                in_=srcap), "mw%d" % pp)
                            cx.wait(ceng, t)
                            if k < 2 and war:
                                for w_ in war:
                                    cx.wait(ceng, w_)
                            dst_ = buf[:, slot * 8192 + d * 1024:
                                       slot * 8192 + (d + 1) * 1024]
                            src_ = wstag[:, pp * 1024:(pp + 1) * 1024]
                            if mat != 3:
                                tok = cx.inc(ceng.tensor_copy(dst_, src_), VQ2)
                            else:
                                tok = cx.inc(ceng.copy(dst_, src_), SQ2)
                            ctoks.append(tok)
                    return vtoks[-1], stoks[-1]

                phases = [(e, hh) for e in range(E) for hh in range(2)]
                wtoks = {}
                wtoks[0] = load_whalf(0, 0, 0, [rt_done])
                for pi, (e, hh) in enumerate(phases):
                    slot = pi % 2
                    if pi + 1 < len(phases):
                        en, hn = phases[pi + 1]
                        war = [g[("wslot", (pi + 1) % 2)]] if ("wslot", (pi + 1) % 2) in g else None
                        wtoks[pi + 1] = load_whalf(en, hn, (pi + 1) % 2, war)
                    vtk, stk_ = wtoks[pi]
                    cx.wait(nc.tensor, vtk)
                    cx.wait(nc.tensor, stk_)
                    first = (pi == 0)
                    for ts in range(8):
                        lastmm = None
                        for ht in range(8):
                            par = ht % 2
                            pt = pt12a if par == 0 else pt12b
                            if ("t12", par) in g:
                                cx.wait(nc.tensor, g[("t12", par)])
                            mt = None
                            for d in range(ND):
                                mt = nc.tensor.matmul(
                                    pt[:, 0:256],
                                    fw1[:, slot * 8192 + d * 1024 + ht * 128:
                                        slot * 8192 + d * 1024 + (ht + 1) * 128],
                                    sb_h2T[:, d * S + ts * 256:d * S + (ts + 1) * 256],
                                    start=(d == 0), stop=(d == ND - 1))
                            for d in range(ND):
                                mt = nc.tensor.matmul(
                                    pt[:, 256:512],
                                    fw2[:, slot * 8192 + d * 1024 + ht * 128:
                                        slot * 8192 + d * 1024 + (ht + 1) * 128],
                                    sb_h2T[:, d * S + ts * 256:d * S + (ts + 1) * 256],
                                    start=(d == 0), stop=(d == ND - 1))
                            mtok = cx.inc(mt, TQ2)
                            cx.wait(nc.scalar, mtok)
                            sct = cx.inc(nc.scalar.copy(sb_t1[:], pt[:, 0:256]), SQ2)
                            cx.wait(nc.vector, sct)
                            prt = cx.inc(nc.vector.scalar_tensor_tensor(
                                sb_pr[:], sb_t1[:], 1.0, pt[:, 256:512],
                                ALU.mult, ALU.mult), VQ2)
                            g[("t12", par)] = prt
                            cx.wait(nc.scalar, prt)
                            if ("hs", par) in g:
                                cx.wait(nc.scalar, g[("hs", par)])
                            gtk = cx.inc(nc.scalar.activation(
                                sb_hs[:, par * 256:(par + 1) * 256], sb_pr[:],
                                AF.Gelu), SQ2)
                            cx.wait(nc.tensor, gtk)
                            mt = None
                            for half in range(2):
                                pyp = py0 if half == 0 else py1
                                for c2 in range(2):
                                    mt = nc.tensor.matmul(
                                        pyp[:, c2 * 512:(c2 + 1) * 512],
                                        sb_hs[:, par * 256 + half * 128:
                                              par * 256 + (half + 1) * 128],
                                        fw3[:, slot * 8192 + ht * 1024 + c2 * 512:
                                            slot * 8192 + ht * 1024 + (c2 + 1) * 512],
                                        start=(ht == 0), stop=(ht == ND - 1))
                            lastmm = cx.inc(mt, TQ2)
                            g[("hs", par)] = lastmm
                        cx.wait(nc.vector, lastmm)
                        for half in range(2):
                            tt = ts * 2 + half
                            pyp = py0 if half == 0 else py1
                            aec = sb_ae[:, e * NT + tt:e * NT + tt + 1]
                            acct = sb_acc[:, tt * D:(tt + 1) * D]
                            if first:
                                atk = cx.inc(nc.vector.tensor_scalar(
                                    acct, pyp[:], aec, None, ALU.mult), VQ2)
                            else:
                                atk = cx.inc(nc.vector.scalar_tensor_tensor(
                                    acct, pyp[:], aec, acct, ALU.mult, ALU.add),
                                    VQ2)
                            cx.wait(nc.vector, atk)
                        g[("t12", 0)] = atk
                        g[("t12", 1)] = atk
                        cx.wait(nc.tensor, atk)
                    g[("wslot", slot)] = cx.tok(TQ2)
                moe_done_v = cx.tok(VQ2)

            with ExitStack() as fsc:
                ptr = fsc.enter_context(nc.psum_tensor("ptr", [128, 512], BF16))
                ptr2 = fsc.enter_context(nc.psum_tensor("ptr2", [128, 512], BF16))
                sb_h2n = fsc.enter_context(nc.sbuf_tensor("sb_h2n", [128, D], BF16))
                sb_sin = fsc.enter_context(nc.sbuf_tensor("sb_sin", [128, 2 * D], BF16))
                sb_pf = fsc.enter_context(nc.sbuf_tensor("sb_pf", [128, D], F32))
                sb_sq = fsc.enter_context(nc.sbuf_tensor("sb_sq2", [128, D], F32))
                sb_of = fsc.enter_context(nc.sbuf_tensor("sb_of", [128, 2 * D], F32))
                stoks = [None] * NT
                moe_done_t = cx.tok(TQ2)
                moe_done_s = cx.tok(SQ2)
                cx.wait(nc.sync, moe_done_t)
                cx.wait(nc.sync, moe_done_v)
                cx.wait(nc.sync, moe_done_s)
                for i in range(2):
                    stoks[i] = cx.dinc(nc.sync.dma_start(
                        out=sb_sin[:, (i % 2) * D:(i % 2 + 1) * D],
                        in_=s_d[i * 128:(i + 1) * 128, :]), "sl%d" % (i % 2))
                cx.wait(nc.tensor, moe_done_v)
                cx.wait(nc.tensor, moe_done_s)
                out_toks = []
                for tt in range(NT):
                    for gi in range(2):
                        pt = ptr if gi % 2 == 0 else ptr2
                        if ("ftr", gi % 2) in g:
                            cx.wait(nc.tensor, g[("ftr", gi % 2)])
                        mt = None
                        for j in range(4):
                            d = gi * 4 + j
                            mt = nc.tensor.transpose(
                                pt[:, j * 128:(j + 1) * 128],
                                sb_h2T[:, d * S + tt * 128:d * S + (tt + 1) * 128],
                                c_identb[:])
                        mtok = cx.inc(mt, TQ2)
                        cx.wait(nc.scalar, mtok)
                        stk2 = nc.scalar.copy(
                            sb_h2n[:, gi * 512:(gi + 1) * 512], pt[:])
                        g[("ftr", gi % 2)] = cx.inc(stk2, SQ2)
                    h2n_done = cx.tok(SQ2)
                    cx.wait(nc.vector, h2n_done)
                    cx.wait(nc.vector, stoks[tt])
                    vw(nc.vector.scalar_tensor_tensor(
                        sb_pf[:], sb_h2n[:], sb_bt[:, tt:tt + 1],
                        sb_acc[:, tt * D:(tt + 1) * D], ALU.mult, ALU.add))
                    sfree = vw(nc.vector.tensor_tensor(
                        sb_pf[:], sb_pf[:],
                        sb_sin[:, (tt % 2) * D:(tt % 2 + 1) * D], ALU.add))
                    if tt + 2 < NT:
                        cx.wait(nc.sync, sfree)
                        stoks[tt + 2] = cx.dinc(nc.sync.dma_start(
                            out=sb_sin[:, ((tt + 2) % 2) * D:((tt + 2) % 2 + 1) * D],
                            in_=s_d[(tt + 2) * 128:(tt + 3) * 128, :]),
                            "sl%d" % (tt % 2))
                    vw(nc.vector.reduce_sum(st_[:, 13:14], sb_pf[:], axis=AX.X))
                    vw(nc.vector.tensor_scalar_mul(st_[:, 14:15], st_[:, 13:14],
                                                   1.0 / D))
                    vw(nc.vector.tensor_scalar(sb_pf[:], sb_pf[:], st_[:, 14:15],
                                               None, ALU.subtract))
                    vt = cx.inc(nc.vector.scalar_tensor_tensor(
                        sb_sq[:], sb_pf[:], 1.0, sb_pf[:], ALU.mult, ALU.mult,
                        accum_out=st_[:, 15:16]), VQ2)
                    cx.wait(nc.scalar, vt)
                    stk2 = cx.inc(nc.scalar.activation(
                        st_[:, 13:14], st_[:, 15:16], AF.Sqrt,
                        bias=c_eps5[:, 0:1], scale=1.0 / D), SQ2)
                    cx.wait(nc.vector, stk2)
                    vw(nc.vector.reciprocal(st_[:, 14:15], st_[:, 13:14]))
                    if ("ost", tt % 2) in g:
                        cx.wait(nc.vector, g[("ost", tt % 2)])
                    vt = cx.inc(nc.vector.tensor_scalar_mul(
                        sb_of[:, (tt % 2) * D:(tt % 2 + 1) * D], sb_pf[:],
                        st_[:, 14:15]), VQ2)
                    cx.wait(nc.gpsimd, vt)
                    otk = cx.dinc(nc.gpsimd.dma_start(
                        out=o_out[tt * 128:(tt + 1) * 128, :],
                        in_=sb_of[:, (tt % 2) * D:(tt % 2 + 1) * D]),
                        "os%d" % (tt % 2))
                    g[("ost", tt % 2)] = otk
                    out_toks.append(otk)
                for ch in ("os0", "os1"):
                    if ch in cx.cnt:
                        cx.wait(nc.sync, cx.tok(ch))
    return nc


def host_prepare2(inputs):
    f32 = np.float32
    sh = {}
    sh["mw1"] = np.ascontiguousarray(
        np.asarray(inputs["L0_ffn_w1"], f32).reshape(E * D, HID))
    sh["mw2"] = np.ascontiguousarray(
        np.asarray(inputs["L0_ffn_w2"], f32).reshape(E * D, HID))
    sh["mw3"] = np.ascontiguousarray(
        np.asarray(inputs["L0_ffn_w3"], f32).reshape(E * HID, D))
    sh["iota8"] = np.broadcast_to(np.arange(E, dtype=f32), (128, E)).copy()
    sh["identb"] = np.eye(128).astype(BF)
    sh["ones1"] = np.ones((1, 128), f32)
    pms = []
    for c in range(NCORES):
        m = np.zeros((2 * NCORES, 2), f32)
        for cp in range(c):
            m[2 * cp, 0] = 1.0
            m[2 * cp, 1] = 1.0
            m[2 * cp + 1, 1] = 1.0
        pms.append(m)
    return sh, pms


def kernel(**inputs):
    hp = host_prepare(inputs)
    nc1 = build_launch1()
    in_maps = []
    for c in range(NCORES):
        m = dict(hp["shared"])
        m["x"] = hp["x"][c]
        in_maps.append(m)
    res1 = run_bass_kernel_spmd(nc1, in_maps, core_ids=list(range(NCORES)))

    table = np.ascontiguousarray(np.concatenate(
        [np.asarray(res1.results[c]["o_cnt"]) for c in range(NCORES)], axis=0))
    sh2, pms = host_prepare2(inputs)
    nc2 = build_launch2()
    in_maps2 = []
    for c in range(NCORES):
        r = res1.results[c]
        m = dict(sh2)
        m["table"] = table
        m["pmask"] = pms[c]
        m["s_in"] = np.asarray(r["o_s"])
        m["h2T_in"] = np.asarray(r["o_h2T"])
        m["idx0"] = np.asarray(r["o_idx0"])
        m["idx1"] = np.asarray(r["o_idx1"])
        m["sc0"] = np.asarray(r["o_sc0"])
        m["sc1"] = np.asarray(r["o_sc1"])
        m["cs0"] = np.asarray(r["o_csel0"])
        m["cs01"] = np.asarray(r["o_csel01"])
        in_maps2.append(m)
    res2 = run_bass_kernel_spmd(nc2, in_maps2, core_ids=list(range(NCORES)))
    out = np.stack([np.asarray(res2.results[c]["o_out"]) for c in range(NCORES)],
                   axis=0)
    return out


# version 02
# speedup vs baseline: 1.3920x; 1.3920x over previous
import numpy as np
import ml_dtypes
from contextlib import ExitStack

import concourse.bass as bass
import concourse.mybir as mybir
from concourse.bass_utils import run_bass_kernel_spmd

F32 = mybir.dt.float32
BF16 = mybir.dt.bfloat16
AF = mybir.ActivationFunctionType
ALU = mybir.AluOpType
AX = mybir.AxisListType
BF = ml_dtypes.bfloat16

S = 2048
D = 1024
NT = S // 128
ND = D // 128
NCORES = 8
HQ, HKV, HD = 16, 4, 64
E, HID = 8, 2048
CAP = 2048.0
NEG = -1.0e30


class Ctx:

    def __init__(self, nc, es):
        self.nc = nc
        self.es = es
        self.sems = {}
        self.cnt = {}

    def sem(self, name):
        if name not in self.sems:
            self.sems[name] = self.es.enter_context(self.nc.semaphore(name))
            self.cnt[name] = 0
        return self.sems[name]

    def inc(self, instr, name, k=1):
        h = self.sem(name)
        instr.then_inc(h, k)
        self.cnt[name] += k
        return (name, self.cnt[name])

    def dinc(self, instr, name):
        return self.inc(instr, name, 16)

    def wait(self, eng, tok):
        if tok is None:
            return
        name, val = tok
        eng.wait_ge(self.sems[name], val)

    def tok(self, name):
        return (name, self.cnt[name])


DSY, DGP, DSC = "dsy", "dgp", "dsc"
TQ, VQ, SQ, GQ = "tq", "vq", "sq", "gq"


def _stream_weight(nc, cx, dram_ap, stag, wbuf, row0, nrows, col0, ncols,
                   cast_eng, cast_q, prev_war=(), chan="sw"):
    if prev_war is None:
        prev_war = ()
    if prev_war and isinstance(prev_war[0], str):
        prev_war = (prev_war,)
    """Stream rows [row0,row0+nrows) x cols [col0,col0+ncols) of a f32 DRAM
    matrix into wbuf bf16 ([128, ntiles*ncols], tile r at cols r*ncols).
    stag: [128, 2*ncols] f32 ping-pong.  Loads on sync queue; casts on
    cast_eng (token stream cast_q).  prev_war: token guarding wbuf reuse.
    Returns last cast token."""
    R = nrows // 128
    casts = []
    for r in range(R):
        p = r % 2
        if r >= 2:
            cx.wait(nc.sync, casts[r - 2])
        if r < 2:
            for pw in prev_war:
                cx.wait(nc.sync, pw)
        t = cx.dinc(nc.sync.dma_start(
            out=stag[:, p * ncols:(p + 1) * ncols],
            in_=dram_ap[row0 + r * 128:row0 + (r + 1) * 128, col0:col0 + ncols]),
            chan + str(p))
        cx.wait(cast_eng, t)
        casts.append(cx.inc(cast_eng.tensor_copy(
            wbuf[:, r * ncols:(r + 1) * ncols],
            stag[:, p * ncols:(p + 1) * ncols]), cast_q))
    return casts[-1]


def build_launch1(debug=False):
    nc = bass.Bass("TRN2", target_bir_lowering=False)

    x_d = nc.dram_tensor("x", [S, D], F32, kind="ExternalInput").ap()
    wqT_d = nc.dram_tensor("wqT", [D, D], F32, kind="ExternalInput").ap()
    wkT_d = nc.dram_tensor("wkT", [D, D], F32, kind="ExternalInput").ap()
    wvT_d = nc.dram_tensor("wvT", [D, D], F32, kind="ExternalInput").ap()
    woT_d = nc.dram_tensor("woT", [D, D], F32, kind="ExternalInput").ap()
    bq_d = nc.dram_tensor("bq128", [128, ND], F32, kind="ExternalInput").ap()
    bk_d = nc.dram_tensor("bk128", [128, ND], F32, kind="ExternalInput").ap()
    bvbc_d = nc.dram_tensor("bv_bc", [128, D], F32, kind="ExternalInput").ap()
    bobc_d = nc.dram_tensor("bo_bc", [128, D], F32, kind="ExternalInput").ap()
    g1bc_d = nc.dram_tensor("g1_bc", [128, D], F32, kind="ExternalInput").ap()
    g2bc_d = nc.dram_tensor("g2_bc", [128, D], F32, kind="ExternalInput").ap()
    fwq_d = nc.dram_tensor("fwq", [D, D], F32, kind="ExternalInput").ap()
    fwk_d = nc.dram_tensor("fwk", [D, HKV * HD], F32, kind="ExternalInput").ap()
    fwv_d = nc.dram_tensor("fwv", [D, HKV * HD], F32, kind="ExternalInput").ap()
    fwo_d = nc.dram_tensor("fwo", [D, D], F32, kind="ExternalInput").ap()
    cosb_d = nc.dram_tensor("cosb", [128, S], BF16, kind="ExternalInput").ap()
    sinb_d = nc.dram_tensor("sinb", [128, S], BF16, kind="ExternalInput").ap()
    identb_d = nc.dram_tensor("identb", [128, 128], BF16, kind="ExternalInput").ap()
    identf_d = nc.dram_tensor("identf", [128, 128], F32, kind="ExternalInput").ap()
    ident8_d = nc.dram_tensor("ident8", [8, 8], F32, kind="ExternalInput").ap()
    cmask_d = nc.dram_tensor("cmask", [128, 128], F32, kind="ExternalInput").ap()
    triu_d = nc.dram_tensor("triu", [128, 128], F32, kind="ExternalInput").ap()
    iota8_d = nc.dram_tensor("iota8", [128, 8], F32, kind="ExternalInput").ap()
    iom9_d = nc.dram_tensor("iota8m999", [128, 8], F32, kind="ExternalInput").ap()
    onesb_d = nc.dram_tensor("ones_bf", [128, 1], BF16, kind="ExternalInput").ap()
    ones1_d = nc.dram_tensor("ones1", [1, 128], F32, kind="ExternalInput").ap()
    onesc_d = nc.dram_tensor("ones128c", [128, 1], F32, kind="ExternalInput").ap()
    rw_d = nc.dram_tensor("rw", [D, E], F32, kind="ExternalInput").ap()
    rb_d = nc.dram_tensor("rb8", [E, 1], F32, kind="ExternalInput").ap()

    o_s = nc.dram_tensor("o_s", [S, D], BF16, kind="ExternalOutput").ap()
    o_h2T = nc.dram_tensor("o_h2T", [D, S], BF16, kind="ExternalOutput").ap()
    o_idx0 = nc.dram_tensor("o_idx0", [128, NT], F32, kind="ExternalOutput").ap()
    o_idx1 = nc.dram_tensor("o_idx1", [128, NT], F32, kind="ExternalOutput").ap()
    o_sc0 = nc.dram_tensor("o_sc0", [128, NT], F32, kind="ExternalOutput").ap()
    o_sc1 = nc.dram_tensor("o_sc1", [128, NT], F32, kind="ExternalOutput").ap()
    o_cs0 = nc.dram_tensor("o_csel0", [128, NT], F32, kind="ExternalOutput").ap()
    o_cs01 = nc.dram_tensor("o_csel01", [128, NT], F32, kind="ExternalOutput").ap()
    o_cnt = nc.dram_tensor("o_cnt", [2, E], F32, kind="ExternalOutput").ap()
    if debug:
        o_x1 = nc.dram_tensor("o_x1", [128, NT * D], BF16, kind="ExternalOutput").ap()
        o_QT = nc.dram_tensor("o_QT", [128, ND * S], BF16, kind="ExternalOutput").ap()
        o_KT = nc.dram_tensor("o_KT", [128, ND * S], BF16, kind="ExternalOutput").ap()
        o_V = nc.dram_tensor("o_V", [128, NT * D], BF16, kind="ExternalOutput").ap()
        o_xnT = nc.dram_tensor("o_xnT", [128, ND * S], BF16, kind="ExternalOutput").ap()
        o_kT2 = nc.dram_tensor("o_kT2", [128, 2 * S], BF16, kind="ExternalOutput").ap()
        o_v2 = nc.dram_tensor("o_v2", [128, NT * 256], BF16, kind="ExternalOutput").ap()
        o_qT2h = nc.dram_tensor("o_qT2h", [128, ND * 1024], BF16, kind="ExternalOutput").ap()
        o_wcb = nc.dram_tensor("o_wcb", [128, 4096], BF16, kind="ExternalOutput").ap()
        o_w0 = nc.dram_tensor("o_w0", [128, NT * 128], BF16, kind="ExternalOutput").ap()
        o_ob0 = nc.dram_tensor("o_ob0", [128, D], BF16, kind="ExternalOutput").ap()
        o_pre0 = nc.dram_tensor("o_pre0", [128, D], F32, kind="ExternalOutput").ap()

    es = ExitStack()
    cx = Ctx(nc, es)

    def sb(name, p, c, dt=F32):
        return es.enter_context(nc.sbuf_tensor(name, [p, c], dt))

    c_identb = sb("c_identb", 128, 128, BF16)
    c_identf = sb("c_identf", 128, 128, F32)
    c_ident8 = sb("c_ident8", 8, 8, F32)
    c_cmask = sb("c_cmask", 128, 128, F32)
    c_triu = sb("c_triu", 128, 128, F32)
    c_iota8 = sb("c_iota8", 128, 8, F32)
    c_iom9 = sb("c_iom9", 128, 8, F32)
    c_onesb = sb("c_onesb", 128, 1, BF16)
    c_ones1 = sb("c_ones1", 1, 128, F32)
    c_onesc = sb("c_onesc", 128, 1, F32)
    c_bq = sb("c_bq", 128, ND, F32)
    c_bk = sb("c_bk", 128, ND, F32)
    c_bvbc = sb("c_bvbc", 128, D, F32)
    c_bobc = sb("c_bobc", 128, D, F32)
    c_g1bc = sb("c_g1bc", 128, D, F32)
    c_g2bc = sb("c_g2bc", 128, D, F32)
    c_rw = sb("c_rw", 128, ND * E, F32)
    c_rb = sb("c_rb", 8, 1, F32)

    sb_x1 = sb("sb_x1", 128, NT * D, BF16)
    sb_lgtok = sb("sb_lgtok", 128, NT * E, F32)
    sb_idx0 = sb("sb_idx0", 128, NT, F32)
    sb_idx1 = sb("sb_idx1", 128, NT, F32)
    sb_sc0 = sb("sb_sc0", 128, NT, F32)
    sb_sc1 = sb("sb_sc1", 128, NT, F32)
    sb_cs0 = sb("sb_cs0", 128, NT, F32)
    sb_cs01 = sb("sb_cs01", 128, NT, F32)
    sb_run0 = sb("sb_run0", 1, E, F32)
    sb_run1 = sb("sb_run1", 1, E, F32)
    sb_run01 = sb("sb_run01", 1, E, F32)
    st_ = sb("sb_stats", 128, 64, F32)
    c_eps5 = sb("c_eps5", 128, 1, F32)
    c_eps6 = sb("c_eps6", 128, 1, F32)

    store_toks = []

    with es:
        with nc.Block():
            for dst, src in [(c_identb, identb_d), (c_identf, identf_d),
                             (c_ident8, ident8_d),
                             (c_cmask, cmask_d), (c_triu, triu_d),
                             (c_iota8, iota8_d), (c_iom9, iom9_d),
                             (c_onesb, onesb_d), (c_ones1, ones1_d),
                             (c_onesc, onesc_d), (c_bq, bq_d), (c_bk, bk_d),
                             (c_bvbc, bvbc_d), (c_bobc, bobc_d),
                             (c_g1bc, g1bc_d), (c_g2bc, g2bc_d), (c_rb, rb_d)]:
                cx.dinc(nc.sync.dma_start(out=dst[:], in_=src), DSY)
            for r in range(ND):
                cx.dinc(nc.sync.dma_start(out=c_rw[:, r * E:(r + 1) * E],
                                          in_=rw_d[r * 128:(r + 1) * 128, :]), DSY)
            consts_tok = cx.tok(DSY)
            for eng in (nc.vector, nc.scalar, nc.tensor, nc.gpsimd):
                cx.wait(eng, consts_tok)

            nc.vector.memset(c_eps5[:], 1.0e-5)
            nc.vector.memset(c_eps6[:], 1.0e-6)
            nc.vector.memset(sb_run0[:], 0.0)
            nc.vector.memset(sb_run1[:], 0.0)
            nc.vector.memset(sb_run01[:], 0.0)

            g = {}

            def vw(instr):
                tok = cx.inc(instr, VQ)
                cx.wait(nc.vector, tok)
                return tok

            with ExitStack() as aes:
                def asb(name, p, c, dt=F32):
                    return aes.enter_context(nc.sbuf_tensor(name, [p, c], dt))

                sb_QT = asb("sb_QT", 128, ND * S, BF16)
                sb_KT = asb("sb_KT", 128, ND * S, BF16)
                sb_V = asb("sb_V", 128, NT * D, BF16)
                stag = asb("stag", 128, 1024, F32)

                with ExitStack() as a1:
                    sb_xT = a1.enter_context(nc.sbuf_tensor("sb_xT", [128, ND * S], BF16))
                    p_trb = a1.enter_context(nc.psum_tensor("p_trb", [128, 512], BF16))
                    p_trb2 = a1.enter_context(nc.psum_tensor("p_trb2", [128, 512], BF16))
                    p_mm = a1.enter_context(nc.psum_tensor("p_mm", [128, 512], F32))
                    p_mm2 = a1.enter_context(nc.psum_tensor("p_mm2", [128, 512], F32))


                    xtes = ExitStack()
                    xstag = xtes.enter_context(nc.sbuf_tensor("xstag", [128, 2 * D], F32))
                    stagb = xtes.enter_context(nc.sbuf_tensor("stagb", [128, 2 * D], BF16))
                    for tt in range(NT):
                        p = tt % 2
                        if tt >= 2:
                            cx.wait(nc.sync, g[("xcast", tt - 2)])
                        dt_ = cx.dinc(nc.sync.dma_start(
                            out=xstag[:, p * D:(p + 1) * D],
                            in_=x_d[tt * 128:(tt + 1) * 128, :]), "xst%d" % p)
                        if tt >= 2:
                            cx.wait(nc.vector, g[("xtr", tt - 2, 1)])
                        cx.wait(nc.vector, dt_)
                        ct = cx.inc(nc.vector.tensor_copy(
                            stagb[:, p * D:(p + 1) * D],
                            xstag[:, p * D:(p + 1) * D]), VQ)
                        g[("xcast", tt)] = ct
                        for gi in range(2):
                            pt = p_trb if (tt * 2 + gi) % 2 == 0 else p_trb2
                            if tt * 2 + gi >= 2:
                                cx.wait(nc.tensor, g[("xtrc", (tt * 2 + gi) % 2)])
                            cx.wait(nc.tensor, ct)
                            mt = None
                            for j in range(4):
                                d = gi * 4 + j
                                mt = nc.tensor.transpose(
                                    pt[:, j * 128:(j + 1) * 128],
                                    stagb[:, p * D + d * 128:p * D + (d + 1) * 128],
                                    c_identb[:])
                            mtok = cx.inc(mt, TQ)
                            g[("xtr", tt, gi)] = mtok
                            cx.wait(nc.scalar, mtok)
                            sctk = None
                            for j in range(4):
                                d = gi * 4 + j
                                sctk = nc.scalar.copy(
                                    sb_xT[:, d * S + tt * 128:d * S + (tt + 1) * 128],
                                    pt[:, j * 128:(j + 1) * 128])
                            g[("xtrc", (tt * 2 + gi) % 2)] = cx.inc(sctk, SQ)
                    xT_done = cx.tok(SQ)

                    wcb = xtes.enter_context(nc.sbuf_tensor("wcb", [128, 4096], BF16))
                    prev_cast = [None]

                    def load_colblock(dram, col0, ncols, base, war_tok):
                        casts = []
                        for d in range(ND):
                            pp = d % 2
                            if d >= 2:
                                cx.wait(nc.sync, casts[d - 2])
                            else:
                                if prev_cast[0] is not None:
                                    cx.wait(nc.sync, prev_cast[0])
                                for wt_ in (war_tok or []):
                                    cx.wait(nc.sync, wt_)
                            t = cx.dinc(nc.sync.dma_start(
                                out=stag[:, pp * ncols:(pp + 1) * ncols],
                                in_=dram[d * 128:(d + 1) * 128,
                                         col0:col0 + ncols]), "cb%d" % pp)
                            cx.wait(nc.vector, t)
                            if d == 0:
                                for wt_ in (war_tok or []):
                                    cx.wait(nc.vector, wt_)
                            casts.append(cx.inc(nc.vector.tensor_copy(
                                wcb[:, base + d * ncols:base + (d + 1) * ncols],
                                stag[:, pp * ncols:(pp + 1) * ncols]), VQ))
                        prev_cast[0] = casts[-1]
                        return casts[-1]

                    stag_war = [g[("xcast", NT - 1)]]
                    for proj, (outb, bias, dram) in enumerate(
                            [(sb_QT, c_bq, wqT_d), (sb_KT, c_bk, wkT_d)]):
                        for m in range(ND):
                            slot = m % 2
                            war = [g[("wcb", slot)]] if ("wcb", slot) in g else None
                            wtok = load_colblock(dram, m * 128, 128, slot * 1024,
                                                 (war or []) + stag_war
                                                 if (war or stag_war) else None)
                            stag_war = []
                            cx.wait(nc.tensor, wtok)
                            if proj == 0 and m == 0:
                                cx.wait(nc.tensor, xT_done)
                            for nch in range(4):
                                kk = (m * 4 + nch) % 2
                                pm = p_mm if kk == 0 else p_mm2
                                if ("qke", kk) in g:
                                    cx.wait(nc.tensor, g[("qke", kk)])
                                mt = None
                                for d in range(ND):
                                    mt = nc.tensor.matmul(
                                        pm[:],
                                        wcb[:, slot * 1024 + d * 128:
                                            slot * 1024 + (d + 1) * 128],
                                        sb_xT[:, d * S + nch * 512:d * S + (nch + 1) * 512],
                                        start=(d == 0), stop=(d == ND - 1))
                                mtok = cx.inc(mt, TQ)
                                if nch == 3:
                                    g[("wcb", slot)] = mtok
                                cx.wait(nc.scalar, mtok)
                                stk = nc.scalar.activation(
                                    outb[:, m * S + nch * 512:m * S + (nch + 1) * 512],
                                    pm[:], AF.Identity, bias=bias[:, m:m + 1], scale=1.0)
                                g[("qke", kk)] = cx.inc(stk, SQ)
                    for nch in range(2):
                        war = [g[k] for k in (("wcb", 0), ("wcb", 1), ("vwcb",))
                               if k in g]
                        wtok = load_colblock(wvT_d, nch * 512, 512, 0, war)
                        cx.wait(nc.tensor, wtok)
                        if debug and nch == 1:
                            cx.wait(nc.gpsimd, wtok)
                            store_toks.append(cx.dinc(nc.gpsimd.dma_start(
                                out=o_wcb, in_=wcb[:]), DGP))
                        for tt in range(NT):
                            kk = tt % 2
                            pm = p_mm if kk == 0 else p_mm2
                            if ("ve", kk) in g:
                                cx.wait(nc.tensor, g[("ve", kk)])
                            elif ("qke", kk) in g:
                                cx.wait(nc.tensor, g.pop(("qke", kk)))
                            mt = None
                            for d in range(ND):
                                mt = nc.tensor.matmul(
                                    pm[:],
                                    sb_xT[:, d * S + tt * 128:d * S + (tt + 1) * 128],
                                    wcb[:, d * 512:(d + 1) * 512],
                                    start=(d == 0), stop=(d == ND - 1))
                            mtok = cx.inc(mt, TQ)
                            if tt == NT - 1:
                                g[("vwcb",)] = mtok
                            cx.wait(nc.vector, mtok)
                            vtk = nc.vector.tensor_tensor(
                                sb_V[:, tt * D + nch * 512:tt * D + (nch + 1) * 512],
                                pm[:], c_bvbc[:, nch * 512:(nch + 1) * 512], ALU.add)
                            g[("ve", kk)] = cx.inc(vtk, VQ)
                    qkv_done_v = cx.tok(VQ)
                    qkv_done_s = cx.tok(SQ)
                    xtes.close()
                    if debug:
                        cx.wait(nc.gpsimd, qkv_done_v)
                        cx.wait(nc.gpsimd, qkv_done_s)
                        store_toks.append(cx.dinc(nc.gpsimd.dma_start(out=o_QT, in_=sb_QT[:]), "dbg"))
                        store_toks.append(cx.dinc(nc.gpsimd.dma_start(out=o_KT, in_=sb_KT[:]), "dbg"))
                        store_toks.append(cx.dinc(nc.gpsimd.dma_start(out=o_V, in_=sb_V[:]), "dbg"))

                with ExitStack() as a2:
                    def a2sb(name, p, c, dt=F32):
                        return a2.enter_context(nc.sbuf_tensor(name, [p, c], dt))
                    p_s = a2.enter_context(nc.psum_tensor("p_s", [128, 512], F32))
                    p_s2 = a2.enter_context(nc.psum_tensor("p_s2", [128, 512], F32))
                    p_tr = a2.enter_context(nc.psum_tensor("p_tr", [128, 512], BF16))
                    p_tr2 = a2.enter_context(nc.psum_tensor("p_tr2", [128, 512], BF16))
                    p_o = a2.enter_context(nc.psum_tensor("p_o", [128, 1024], F32))
                    p_n = a2.enter_context(nc.psum_tensor("p_n", [128, 8], F32))
                    p_2 = a2.enter_context(nc.psum_tensor("p_2", [128, 512], F32))
                    sb_w = a2sb("sb_w", 128, NT * 128, BF16)
                    sb_wT = a2sb("sb_wT", 128, NT * 128, BF16)
                    sb_oT = a2sb("sb_oT", 128, ND * 128, BF16)
                    sb_ob = a2sb("sb_ob", 128, D, BF16)
                    sb_xr = a2sb("sb_xr", 128, 2 * D, F32)
                    sb_pre = a2sb("sb_pre", 128, D, F32)
                    sb_sq = a2sb("sb_sq", 128, D, F32)
                    woB = a2sb("woB", 128, ND * D, BF16)
                    stag2 = a2sb("stag2", 128, 2 * D, F32)

                    wtokO = _stream_weight(nc, cx, woT_d, stag2, woB, 0, D, 0, D,
                                           nc.vector, VQ,
                                           prev_war=[cx.tok(TQ)], chan="ow")
                    cx.wait(nc.tensor, wtokO)
                    cx.wait(nc.tensor, qkv_done_v)
                    cx.wait(nc.tensor, qkv_done_s)

                    xr_tok = [None] * NT
                    cx.wait(nc.gpsimd, cx.tok(TQ))
                    for i in range(2):
                        xr_tok[i] = cx.dinc(nc.gpsimd.dma_start(
                            out=sb_xr[:, (i % 2) * D:(i % 2 + 1) * D],
                            in_=x_d[i * 128:(i + 1) * 128, :]), "xr%d" % (i % 2))

                    for i in range(NT):
                        for kc in range(4):
                            pm = p_s if kc % 2 == 0 else p_s2
                            if ("sev", kc % 2) in g:
                                cx.wait(nc.tensor, g[("sev", kc % 2)])
                            if i > 0 and kc == 0:
                                cx.wait(nc.scalar, g[("wtr",)])
                                cx.wait(nc.vector, g[("wtr",)])
                            mt = None
                            for d in range(ND):
                                mt = nc.tensor.matmul(
                                    pm[:],
                                    sb_QT[:, d * S + i * 128:d * S + (i + 1) * 128],
                                    sb_KT[:, d * S + kc * 512:d * S + (kc + 1) * 512],
                                    start=(d == 0), stop=(d == ND - 1))
                            mtok = cx.inc(mt, TQ)
                            for j in range(4):
                                kt = kc * 4 + j
                                cx.wait(nc.vector, mtok)
                                vw(nc.vector.reduce_max(
                                    st_[:, 0:1], pm[:, j * 128:(j + 1) * 128],
                                    axis=AX.X, negate=True))
                                vt0 = cx.inc(nc.vector.tensor_scalar_mul(
                                    st_[:, 1:2], st_[:, 0:1], 1.0 / 32.0), VQ)
                                cx.wait(nc.scalar, vt0)
                                stk = cx.inc(nc.scalar.activation(
                                    sb_w[:, kt * 128:(kt + 1) * 128],
                                    pm[:, j * 128:(j + 1) * 128], AF.Exp,
                                    bias=st_[:, 1:2], scale=1.0 / 32.0), SQ)
                                if j == 3:
                                    g[("sev", kc % 2)] = stk
                                cx.wait(nc.vector, stk)
                                vw(nc.vector.reduce_sum(
                                    st_[:, 2:3],
                                    sb_w[:, kt * 128:(kt + 1) * 128], axis=AX.X))
                                vw(nc.vector.reciprocal(st_[:, 3:4], st_[:, 2:3]))
                                cx.inc(nc.vector.tensor_scalar_mul(
                                    sb_w[:, kt * 128:(kt + 1) * 128],
                                    sb_w[:, kt * 128:(kt + 1) * 128],
                                    st_[:, 3:4]), VQ)
                        wnorm_done = cx.tok(VQ)
                        if debug and i == 0:
                            cx.wait(nc.gpsimd, wnorm_done)
                            dtk = cx.dinc(nc.gpsimd.dma_start(out=o_w0, in_=sb_w[:]), "dbg")
                            cx.wait(nc.tensor, dtk)
                            cx.wait(nc.vector, dtk)
                            cx.wait(nc.scalar, dtk)
                        for gi in range(4):
                            pt = p_tr if gi % 2 == 0 else p_tr2
                            if ("trc", gi % 2) in g:
                                cx.wait(nc.tensor, g[("trc", gi % 2)])
                            cx.wait(nc.tensor, wnorm_done)
                            if i > 0 and gi == 0:
                                cx.wait(nc.tensor, g[("pvmm",)])
                            mt = None
                            for j in range(4):
                                kt = gi * 4 + j
                                mt = nc.tensor.transpose(
                                    pt[:, j * 128:(j + 1) * 128],
                                    sb_w[:, kt * 128:(kt + 1) * 128], c_identb[:])
                            mtok = cx.inc(mt, TQ)
                            if gi == 3:
                                g[("wtr",)] = mtok
                            cx.wait(nc.scalar, mtok)
                            stk = nc.scalar.copy(sb_wT[:, gi * 512:(gi + 1) * 512],
                                                 pt[:])
                            g[("trc", gi % 2)] = cx.inc(stk, SQ)
                        wT_done = cx.tok(SQ)
                        cx.wait(nc.tensor, wT_done)
                        if ("pvev",) in g:
                            cx.wait(nc.tensor, g[("pvev",)])
                        mt = None
                        for kt in range(NT):
                            nc.tensor.matmul(
                                p_o[:, 0:512],
                                sb_wT[:, kt * 128:(kt + 1) * 128],
                                sb_V[:, kt * D:kt * D + 512],
                                start=(kt == 0), stop=(kt == NT - 1))
                            nc.tensor.matmul(
                                p_o[:, 512:1024],
                                sb_wT[:, kt * 128:(kt + 1) * 128],
                                sb_V[:, kt * D + 512:(kt + 1) * D],
                                start=(kt == 0), stop=(kt == NT - 1))
                            mt = nc.tensor.matmul(
                                p_n[:, 0:1],
                                sb_wT[:, kt * 128:(kt + 1) * 128],
                                c_onesb[:],
                                start=(kt == 0), stop=(kt == NT - 1))
                        mtok = cx.inc(mt, TQ)
                        g[("pvmm",)] = mtok
                        cx.wait(nc.vector, mtok)
                        if i > 0:
                            cx.wait(nc.vector, g[("obr",)])
                        vw(nc.vector.tensor_scalar_add(st_[:, 4:5], p_n[:, 0:1],
                                                       1.0e-6))
                        vw(nc.vector.reciprocal(st_[:, 5:6], st_[:, 4:5]))
                        vt = cx.inc(nc.vector.tensor_scalar_mul(
                            sb_ob[:], p_o[:], st_[:, 5:6]), VQ)
                        g[("pvev",)] = vt
                        if debug and i == 0:
                            cx.wait(nc.gpsimd, vt)
                            dtk = cx.dinc(nc.gpsimd.dma_start(out=o_ob0, in_=sb_ob[:]), "dbg")
                            cx.wait(nc.tensor, dtk)
                            cx.wait(nc.vector, dtk)
                            cx.wait(nc.scalar, dtk)
                        for gi in range(2):
                            pt = p_tr if gi % 2 == 0 else p_tr2
                            cx.wait(nc.tensor, g[("trc", gi % 2)])
                            cx.wait(nc.tensor, vt)
                            if i > 0 and gi == 0:
                                cx.wait(nc.tensor, g[("o2mm",)])
                            mt = None
                            for j in range(4):
                                d = gi * 4 + j
                                mt = nc.tensor.transpose(
                                    pt[:, j * 128:(j + 1) * 128],
                                    sb_ob[:, d * 128:(d + 1) * 128], c_identb[:])
                            mtok = cx.inc(mt, TQ)
                            if gi == 1:
                                g[("obr",)] = mtok
                            cx.wait(nc.scalar, mtok)
                            stk = nc.scalar.copy(sb_oT[:, gi * 512:(gi + 1) * 512],
                                                 pt[:])
                            g[("trc", gi % 2)] = cx.inc(stk, SQ)
                        oT_done = cx.tok(SQ)
                        cx.wait(nc.tensor, oT_done)
                        for ch in range(2):
                            if ("o2ev",) in g:
                                cx.wait(nc.tensor, g[("o2ev",)])
                            mt = None
                            for d in range(ND):
                                mt = nc.tensor.matmul(
                                    p_2[:],
                                    sb_oT[:, d * 128:(d + 1) * 128],
                                    woB[:, d * D + ch * 512:d * D + (ch + 1) * 512],
                                    start=(d == 0), stop=(d == ND - 1))
                            mtok = cx.inc(mt, TQ)
                            if ch == 1:
                                g[("o2mm",)] = mtok
                            cx.wait(nc.vector, mtok)
                            vt = cx.inc(nc.vector.tensor_tensor(
                                sb_pre[:, ch * 512:(ch + 1) * 512], p_2[:],
                                c_bobc[:, ch * 512:(ch + 1) * 512], ALU.add), VQ)
                            g[("o2ev",)] = vt
                        cx.wait(nc.vector, xr_tok[i])
                        vt = cx.inc(nc.vector.tensor_tensor(
                            sb_pre[:], sb_pre[:],
                            sb_xr[:, (i % 2) * D:(i % 2 + 1) * D], ALU.add), VQ)
                        if i + 2 < NT:
                            cx.wait(nc.gpsimd, vt)
                            xr_tok[i + 2] = cx.dinc(nc.gpsimd.dma_start(
                                out=sb_xr[:, ((i + 2) % 2) * D:((i + 2) % 2 + 1) * D],
                                in_=x_d[(i + 2) * 128:(i + 3) * 128, :]), "xr%d" % (i % 2))
                        if debug and i == 0:
                            dtk0 = cx.inc(nc.vector.tensor_copy(sb_sq[:], sb_pre[:]), VQ)
                            cx.wait(nc.gpsimd, dtk0)
                            dtk = cx.dinc(nc.gpsimd.dma_start(out=o_pre0, in_=sb_sq[:]), "dbg")
                            cx.wait(nc.vector, dtk)
                        vw(nc.vector.reduce_sum(st_[:, 6:7], sb_pre[:], axis=AX.X))
                        vw(nc.vector.tensor_scalar_mul(st_[:, 7:8], st_[:, 6:7],
                                                       1.0 / D))
                        vw(nc.vector.tensor_scalar(sb_pre[:], sb_pre[:],
                                                   st_[:, 7:8], None, ALU.subtract))
                        vt = cx.inc(nc.vector.scalar_tensor_tensor(
                            sb_sq[:], sb_pre[:], 1.0, sb_pre[:], ALU.mult, ALU.mult,
                            accum_out=st_[:, 8:9]), VQ)
                        cx.wait(nc.scalar, vt)
                        stk = cx.inc(nc.scalar.activation(
                            st_[:, 9:10], st_[:, 8:9], AF.Sqrt,
                            bias=c_eps5[:, 0:1], scale=1.0 / D), SQ)
                        cx.wait(nc.vector, stk)
                        vw(nc.vector.reciprocal(st_[:, 10:11], st_[:, 9:10]))
                        cx.inc(nc.vector.tensor_scalar_mul(
                            sb_x1[:, i * D:(i + 1) * D], sb_pre[:],
                            st_[:, 10:11]), VQ)
                    a2_done_v = cx.tok(VQ)
                    a2_done_s = cx.tok(SQ)
                    a2_done_t = cx.tok(TQ)
                    if debug:
                        cx.wait(nc.gpsimd, a2_done_v)
                        store_toks.append(cx.dinc(nc.gpsimd.dma_start(out=o_x1, in_=sb_x1[:]), "dbg"))

            with ExitStack() as bes:
                def bsb(name, p, c, dt=F32):
                    return bes.enter_context(nc.sbuf_tensor(name, [p, c], dt))

                sb_xnT = bsb("sb_xnT", 128, ND * S, BF16)
                sb_kT2 = bsb("sb_kT2", 128, 2 * S, BF16)
                sb_v2 = bsb("sb_v2", 128, NT * 256, BF16)
                sb_qT2h = bsb("sb_qT2h", 128, ND * 1024, BF16)
                fw_s = bsb("fw_s", 128, ND * 256, BF16)
                fw_q = bsb("fw_q", 128, ND * D, BF16)
                fw_o = bsb("fw_o", 128, ND * D, BF16)
                stg = bsb("stg", 128, 2 * D, F32)
                sb_f0 = bsb("sb_f0", 128, D, F32)
                sb_f1 = bsb("sb_f1", 128, D, F32)
                sb_f2 = bsb("sb_f2", 128, D, F32)
                sb_nb = bsb("sb_nb", 128, D, BF16)
                sb_pt = bsb("sb_pt", 128, S, BF16)
                sb_ptT = bsb("sb_ptT", 128, NT * 128, BF16)
                sb_oT2 = bsb("sb_oT2", 128, ND * 128, BF16)
                sb_h2fT = bsb("sb_h2fT", 128, ND * 128, F32)
                sb_srng = bsb("sb_srng", 128, 2 * D, BF16)
                sb_h2rng = bsb("sb_h2rng", 128, 2 * D, BF16)
                c_cos = bsb("c_cos", 128, S, BF16)
                c_sin = bsb("c_sin", 128, S, BF16)
                cx.wait(nc.sync, a2_done_t)
                cx.wait(nc.sync, a2_done_s)
                cx.wait(nc.sync, a2_done_v)
                cx.dinc(nc.sync.dma_start(out=c_cos[:], in_=cosb_d), "cld2")
                cs_tok = cx.dinc(nc.sync.dma_start(out=c_sin[:], in_=sinb_d), "cld2")
                cx.wait(nc.vector, cs_tok)

                with ExitStack() as b0:
                    pb_tr = b0.enter_context(nc.psum_tensor("pb_tr", [128, 512], BF16))
                    pb_tr2 = b0.enter_context(nc.psum_tensor("pb_tr2", [128, 512], BF16))
                    pb_mm = b0.enter_context(nc.psum_tensor("pb_mm", [128, 512], F32))
                    pb_mm2 = b0.enter_context(nc.psum_tensor("pb_mm2", [128, 512], F32))

                    cx.wait(nc.vector, a2_done_v)
                    cx.wait(nc.tensor, a2_done_s)
                    cx.wait(nc.tensor, a2_done_v)
                    for tt in range(NT):
                        x1t = sb_x1[:, tt * D:(tt + 1) * D]
                        if tt > 0:
                            cx.wait(nc.vector, g[("xntr", 1)])
                        vt = cx.inc(nc.vector.scalar_tensor_tensor(
                            sb_f0[:], x1t, 1.0, x1t, ALU.mult, ALU.mult,
                            accum_out=st_[:, 0:1]), VQ)
                        cx.wait(nc.scalar, vt)
                        stk = cx.inc(nc.scalar.activation(
                            st_[:, 1:2], st_[:, 0:1], AF.Sqrt,
                            bias=c_eps6[:, 0:1], scale=1.0 / D), SQ)
                        cx.wait(nc.vector, stk)
                        vw(nc.vector.reciprocal(st_[:, 2:3], st_[:, 1:2]))
                        vw(nc.vector.tensor_scalar_mul(sb_f1[:], x1t, st_[:, 2:3]))
                        vt = cx.inc(nc.vector.tensor_tensor(
                            sb_nb[:], sb_f1[:], c_g1bc[:], ALU.mult), VQ)
                        for gi in range(2):
                            pt = pb_tr if gi % 2 == 0 else pb_tr2
                            if ("xntrc", gi % 2) in g:
                                cx.wait(nc.tensor, g[("xntrc", gi % 2)])
                            cx.wait(nc.tensor, vt)
                            mt = None
                            for j in range(4):
                                d = gi * 4 + j
                                mt = nc.tensor.transpose(
                                    pt[:, j * 128:(j + 1) * 128],
                                    sb_nb[:, d * 128:(d + 1) * 128], c_identb[:])
                            mtok = cx.inc(mt, TQ)
                            g[("xntr", gi)] = mtok
                            cx.wait(nc.scalar, mtok)
                            sctk = None
                            for j in range(4):
                                d = gi * 4 + j
                                sctk = nc.scalar.copy(
                                    sb_xnT[:, d * S + tt * 128:d * S + (tt + 1) * 128],
                                    pt[:, j * 128:(j + 1) * 128])
                            g[("xntrc", gi % 2)] = cx.inc(sctk, SQ)
                    xnT_done = cx.tok(SQ)

                    wtok = _stream_weight(nc, cx, fwk_d, stg, fw_s, 0, D, 0, 256,
                                          nc.vector, VQ,
                                          prev_war=[a2_done_t, a2_done_s], chan="bw")
                    cx.wait(nc.tensor, wtok)
                    cx.wait(nc.tensor, xnT_done)
                    for m2 in range(2):
                        for nch in range(4):
                            kk = (m2 * 4 + nch) % 2
                            pm = pb_mm if kk == 0 else pb_mm2
                            if ("k2e", kk) in g:
                                cx.wait(nc.tensor, g[("k2e", kk)])
                            mt = None
                            for d in range(ND):
                                mt = nc.tensor.matmul(
                                    pm[:],
                                    fw_s[:, d * 256 + m2 * 128:d * 256 + (m2 + 1) * 128],
                                    sb_xnT[:, d * S + nch * 512:d * S + (nch + 1) * 512],
                                    start=(d == 0), stop=(d == ND - 1))
                            mtok = cx.inc(mt, TQ)
                            cx.wait(nc.vector, mtok)
                            c0 = nch * 512
                            vw(nc.vector.tensor_copy(sb_f1[:, 0:512], pm[:]))
                            nc.vector.tensor_copy(sb_f2[0:32, 0:512], sb_f1[32:64, 0:512])
                            nc.vector.tensor_copy(sb_f2[32:64, 0:512], sb_f1[0:32, 0:512])
                            nc.vector.tensor_copy(sb_f2[64:96, 0:512], sb_f1[96:128, 0:512])
                            nc.vector.tensor_copy(sb_f2[96:128, 0:512], sb_f1[64:96, 0:512])
                            nc.vector.tensor_tensor(sb_f1[:, 0:512], sb_f1[:, 0:512],
                                                    c_cos[:, c0:c0 + 512], ALU.mult)
                            nc.vector.tensor_tensor(sb_f2[:, 0:512], sb_f2[:, 0:512],
                                                    c_sin[:, c0:c0 + 512], ALU.mult)
                            vt = cx.inc(nc.vector.tensor_tensor(
                                sb_kT2[:, m2 * S + c0:m2 * S + c0 + 512],
                                sb_f1[:, 0:512], sb_f2[:, 0:512], ALU.add), VQ)
                            g[("k2e", kk)] = vt
                    k2_done = cx.tok(VQ)
                    wtokv = _stream_weight(nc, cx, fwv_d, stg, fw_s, 0, D, 0, 256,
                                           nc.vector, VQ,
                                           prev_war=[cx.tok(TQ), wtok], chan="bw")
                    cx.wait(nc.tensor, wtokv)
                    for tt in range(NT):
                        kk = tt % 2
                        pm = pb_mm if kk == 0 else pb_mm2
                        if ("v2e", kk) in g:
                            cx.wait(nc.tensor, g[("v2e", kk)])
                        mt = None
                        for d in range(ND):
                            mt = nc.tensor.matmul(
                                pm[:, 0:256],
                                sb_xnT[:, d * S + tt * 128:d * S + (tt + 1) * 128],
                                fw_s[:, d * 256:(d + 1) * 256],
                                start=(d == 0), stop=(d == ND - 1))
                        mtok = cx.inc(mt, TQ)
                        cx.wait(nc.vector, mtok)
                        vt = cx.inc(nc.vector.tensor_copy(
                            sb_v2[:, tt * 256:(tt + 1) * 256], pm[:, 0:256]), VQ)
                        g[("v2e", kk)] = vt
                    v2_done = cx.tok(VQ)
                    if debug:
                        cx.wait(nc.gpsimd, xnT_done)
                        cx.wait(nc.gpsimd, k2_done)
                        cx.wait(nc.gpsimd, v2_done)
                        store_toks.append(cx.dinc(nc.gpsimd.dma_start(out=o_xnT, in_=sb_xnT[:]), "dbg"))
                        store_toks.append(cx.dinc(nc.gpsimd.dma_start(out=o_kT2, in_=sb_kT2[:]), "dbg"))
                        store_toks.append(cx.dinc(nc.gpsimd.dma_start(out=o_v2, in_=sb_v2[:]), "dbg"))
                    wtokq = _stream_weight(nc, cx, fwq_d, stg, fw_q, 0, D, 0, D,
                                           nc.vector, VQ, prev_war=[wtokv], chan="bw")
                    wtoko = _stream_weight(nc, cx, fwo_d, stg, fw_o, 0, D, 0, D,
                                           nc.vector, VQ, prev_war=[wtokq], chan="bw")
                    b1_done_t = cx.tok(TQ)

                with ExitStack() as b2:
                    preg = [b2.enter_context(nc.psum_tensor(f"preg{r}", [128, 1024], F32))
                            for r in range(3)]
                    ptrb = b2.enter_context(nc.psum_tensor("ptrb", [128, 512], BF16))
                    pvo = b2.enter_context(nc.psum_tensor("pvo", [64, 128], F32))
                    runit = [0]

                    def next_region(eng):
                        r = runit[0] % 3
                        runit[0] += 1
                        if ("reg", r) in g:
                            cx.wait(eng, g[("reg", r)])
                        return r

                    cx.wait(nc.tensor, wtokq)
                    cx.wait(nc.tensor, wtoko)
                    cx.wait(nc.tensor, k2_done)
                    cx.wait(nc.tensor, v2_done)

                    half_done_t = None
                    for hf in range(2):
                        if half_done_t is not None:
                            cx.wait(nc.vector, half_done_t)
                        for m in range(ND):
                            for nch in range(2):
                                r = next_region(nc.tensor)
                                pm = preg[r]
                                c0 = hf * 1024 + nch * 512
                                mt = None
                                for d in range(ND):
                                    mt = nc.tensor.matmul(
                                        pm[:, 0:512],
                                        fw_q[:, d * D + m * 128:d * D + (m + 1) * 128],
                                        sb_xnT[:, d * S + c0:d * S + c0 + 512],
                                        start=(d == 0), stop=(d == ND - 1))
                                mtok = cx.inc(mt, TQ)
                                cx.wait(nc.vector, mtok)
                                if ("qrot",) in g:
                                    cx.wait(nc.vector, g[("qrot",)])
                                vt = cx.inc(nc.vector.tensor_copy(
                                    sb_f1[:, 0:512], pm[:, 0:512]), VQ)
                                cx.wait(nc.vector, vt)
                                g[("reg", r)] = vt
                                nc.vector.tensor_copy(sb_f2[0:32, 0:512],
                                                      sb_f1[32:64, 0:512])
                                nc.vector.tensor_copy(sb_f2[32:64, 0:512],
                                                      sb_f1[0:32, 0:512])
                                nc.vector.tensor_copy(sb_f2[64:96, 0:512],
                                                      sb_f1[96:128, 0:512])
                                nc.vector.tensor_copy(sb_f2[96:128, 0:512],
                                                      sb_f1[64:96, 0:512])
                                nc.vector.tensor_tensor(
                                    sb_f1[:, 0:512], sb_f1[:, 0:512],
                                    c_cos[:, c0:c0 + 512], ALU.mult)
                                nc.vector.tensor_tensor(
                                    sb_f2[:, 0:512], sb_f2[:, 0:512],
                                    c_sin[:, c0:c0 + 512], ALU.mult)
                                vt = cx.inc(nc.vector.tensor_tensor(
                                    sb_qT2h[:, m * 1024 + nch * 512:m * 1024 + (nch + 1) * 512],
                                    sb_f1[:, 0:512], sb_f2[:, 0:512], ALU.add), VQ)
                                g[("qrot",)] = vt
                        qh_done = cx.tok(VQ)
                        cx.wait(nc.tensor, qh_done)

                        for il in range(NT // 2):
                            qt = hf * (NT // 2) + il
                            klen = (qt + 1) * 128
                            nchk = (klen + 1023) // 1024
                            x1t = sb_x1[:, qt * D:(qt + 1) * D]
                            for h in range(HQ):
                                kv = h % HKV
                                ft = h // 2
                                rowlo = (h % 2) * 64
                                qslice = sb_qT2h[rowlo:rowlo + 64,
                                                 ft * 1024 + il * 128:ft * 1024 + (il + 1) * 128]
                                if ("pttr",) in g:
                                    cx.wait(nc.scalar, g[("pttr",)])
                                regs_used = []
                                for c in range(nchk):
                                    lc = min(1024, klen - c * 1024)
                                    r = next_region(nc.tensor)
                                    regs_used.append((r, lc))
                                    pm = preg[r]
                                    mt = None
                                    for sl in range((lc + 511) // 512):
                                        w = min(512, lc - sl * 512)
                                        koff = c * 1024 + sl * 512
                                        m2 = kv // 2
                                        rlo = (kv % 2) * 64
                                        mt = nc.tensor.matmul(
                                            pm[:, sl * 512:sl * 512 + w],
                                            qslice,
                                            sb_kT2[rlo:rlo + 64,
                                                   m2 * S + koff:m2 * S + koff + w],
                                            start=True, stop=True)
                                    mtok = cx.inc(mt, TQ)
                                    cx.wait(nc.vector, mtok)
                                    if (qt * 128) // 1024 == c:
                                        off = qt * 128 - c * 1024
                                        vw(nc.vector.tensor_tensor(
                                            pm[:, off:off + 128], pm[:, off:off + 128],
                                            c_cmask[:], ALU.add))
                                    vw(nc.vector.reduce_max(
                                        st_[:, 12 + c:13 + c], pm[:, 0:lc],
                                        axis=AX.X, negate=True))
                                    vt0 = cx.inc(nc.vector.tensor_scalar_mul(
                                        st_[:, 16 + c:17 + c], st_[:, 12 + c:13 + c],
                                        1.0 / 8.0), VQ)
                                    cx.wait(nc.scalar, vt0)
                                    stk = cx.inc(nc.scalar.activation(
                                        sb_pt[:, c * 1024:c * 1024 + lc],
                                        pm[:, 0:lc], AF.Exp,
                                        bias=st_[:, 16 + c:17 + c],
                                        scale=1.0 / 8.0), SQ)
                                    g[("reg", r)] = stk
                                    cx.wait(nc.vector, stk)
                                    vw(nc.vector.reduce_sum(
                                        st_[:, 20 + c:21 + c],
                                        sb_pt[:, c * 1024:c * 1024 + lc],
                                        axis=AX.X))
                                cx.wait(nc.vector, cx.tok(SQ))
                                if nchk == 1:
                                    vw(nc.vector.reciprocal(st_[:, 25:26],
                                                            st_[:, 20:21]))
                                    vsc = cx.inc(nc.vector.tensor_scalar_mul(
                                        sb_pt[:, 0:klen], sb_pt[:, 0:klen],
                                        st_[:, 25:26]), VQ)
                                else:
                                    vw(nc.vector.tensor_tensor(
                                        st_[:, 28:29], st_[:, 12:13], st_[:, 13:14],
                                        ALU.min))
                                    for c in range(2):
                                        vw(nc.vector.tensor_tensor(
                                            st_[:, 29:30], st_[:, 28:29],
                                            st_[:, 12 + c:13 + c], ALU.subtract))
                                        vt0 = cx.inc(nc.vector.tensor_scalar_mul(
                                            st_[:, 29 + 10 * c:30 + 10 * c],
                                            st_[:, 29:30], 1.0 / 8.0), VQ)
                                        cx.wait(nc.vector, vt0)
                                        cx.wait(nc.scalar, vt0)
                                        stk = cx.inc(nc.scalar.activation(
                                            st_[:, 30 + c:31 + c],
                                            st_[:, 29 + 10 * c:30 + 10 * c],
                                            AF.Exp), SQ)
                                        cx.wait(nc.vector, stk)
                                    vw(nc.vector.tensor_tensor(
                                        st_[:, 32:33], st_[:, 30:31], st_[:, 20:21],
                                        ALU.mult))
                                    vw(nc.vector.scalar_tensor_tensor(
                                        st_[:, 33:34], st_[:, 21:22], 1.0,
                                        st_[:, 31:32], ALU.mult, ALU.mult))
                                    vw(nc.vector.tensor_tensor(
                                        st_[:, 24:25], st_[:, 32:33], st_[:, 33:34],
                                        ALU.add))
                                    vw(nc.vector.reciprocal(st_[:, 25:26],
                                                            st_[:, 24:25]))
                                    vw(nc.vector.tensor_tensor(
                                        st_[:, 34:35], st_[:, 30:31], st_[:, 25:26],
                                        ALU.mult))
                                    vw(nc.vector.tensor_tensor(
                                        st_[:, 35:36], st_[:, 31:32], st_[:, 25:26],
                                        ALU.mult))
                                    nc.vector.tensor_scalar_mul(
                                        sb_pt[:, 0:1024], sb_pt[:, 0:1024],
                                        st_[:, 34:35])
                                    vsc = cx.inc(nc.vector.tensor_scalar_mul(
                                        sb_pt[:, 1024:klen], sb_pt[:, 1024:klen],
                                        st_[:, 35:36]), VQ)
                                nkt = qt + 1
                                cx.wait(nc.tensor, vsc)
                                if ("ptTw",) in g:
                                    cx.wait(nc.tensor, g[("ptTw",)])
                                for gi in range((nkt + 3) // 4):
                                    if ("btrc",) in g:
                                        cx.wait(nc.tensor, g[("btrc",)])
                                    mt = None
                                    nj = min(4, nkt - gi * 4)
                                    for j in range(nj):
                                        kt = gi * 4 + j
                                        mt = nc.tensor.transpose(
                                            ptrb[:, j * 128:(j + 1) * 128],
                                            sb_pt[:, kt * 128:(kt + 1) * 128],
                                            c_identb[:])
                                    mtok = cx.inc(mt, TQ)
                                    g[("pttr",)] = mtok
                                    cx.wait(nc.scalar, mtok)
                                    stk = nc.scalar.copy(
                                        sb_ptT[:, gi * 512:gi * 512 + nj * 128],
                                        ptrb[:, 0:nj * 128])
                                    g[("btrc",)] = cx.inc(stk, SQ)
                                ptT_done = cx.tok(SQ)
                                cx.wait(nc.tensor, ptT_done)
                                if ("pvoe",) in g:
                                    cx.wait(nc.tensor, g[("pvoe",)])
                                mt = None
                                for kt in range(nkt):
                                    mt = nc.tensor.matmul(
                                        pvo[:],
                                        sb_v2[:, kt * 256 + kv * 64:kt * 256 + (kv + 1) * 64],
                                        sb_ptT[:, kt * 128:(kt + 1) * 128],
                                        start=(kt == 0), stop=(kt == nkt - 1))
                                mtok = cx.inc(mt, TQ)
                                g[("ptTw",)] = mtok
                                cx.wait(nc.vector, mtok)
                                if ("oT2w",) in g and h == 0:
                                    cx.wait(nc.vector, g[("oT2w",)])
                                vt = cx.inc(nc.vector.tensor_copy(
                                    sb_oT2[rowlo:rowlo + 64, ft * 128:(ft + 1) * 128],
                                    pvo[0:64, :]), VQ)
                                g[("pvoe",)] = vt
                            heads_done = cx.tok(VQ)
                            cx.wait(nc.tensor, heads_done)
                            for ch in range(2):
                                r = next_region(nc.tensor)
                                pm = preg[r]
                                mt = None
                                for d in range(ND):
                                    mt = nc.tensor.matmul(
                                        pm[:, 0:512],
                                        sb_oT2[:, d * 128:(d + 1) * 128],
                                        fw_o[:, d * D + ch * 512:d * D + (ch + 1) * 512],
                                        start=(d == 0), stop=(d == ND - 1))
                                mtok = cx.inc(mt, TQ)
                                if ch == 1:
                                    g[("oT2w",)] = mtok
                                cx.wait(nc.vector, mtok)
                                vt = cx.inc(nc.vector.tensor_tensor(
                                    sb_f1[:, ch * 512:(ch + 1) * 512], pm[:, 0:512],
                                    x1t[:, ch * 512:(ch + 1) * 512], ALU.add), VQ)
                                g[("reg", r)] = vt
                            if ("sst", qt % 2) in g:
                                cx.wait(nc.vector, g[("sst", qt % 2)])
                            vt = cx.inc(nc.vector.tensor_tensor(
                                sb_srng[:, (qt % 2) * D:(qt % 2 + 1) * D],
                                sb_f1[:], x1t, ALU.add), VQ)
                            cx.wait(nc.gpsimd, vt)
                            stok = cx.dinc(nc.gpsimd.dma_start(
                                out=o_s[qt * 128:(qt + 1) * 128, :],
                                in_=sb_srng[:, (qt % 2) * D:(qt % 2 + 1) * D]),
                                "ss%d" % (qt % 2))
                            g[("sst", qt % 2)] = stok
                            store_toks.append(stok)
                            if ("lgf0",) in g:
                                cx.wait(nc.vector, g[("lgf0",)])
                            vt = cx.inc(nc.vector.scalar_tensor_tensor(
                                sb_f0[:], sb_f1[:], 1.0, sb_f1[:], ALU.mult,
                                ALU.mult, accum_out=st_[:, 36:37]), VQ)
                            cx.wait(nc.scalar, vt)
                            stk = cx.inc(nc.scalar.activation(
                                st_[:, 37:38], st_[:, 36:37], AF.Sqrt,
                                bias=c_eps6[:, 0:1], scale=1.0 / D), SQ)
                            cx.wait(nc.vector, stk)
                            vw(nc.vector.reciprocal(st_[:, 38:39], st_[:, 37:38]))
                            vw(nc.vector.tensor_scalar_mul(sb_f0[:], sb_f1[:],
                                                           st_[:, 38:39]))
                            vt = cx.inc(nc.vector.tensor_tensor(
                                sb_f2[:], sb_f0[:], c_g2bc[:], ALU.mult), VQ)
                            for gi in range(2):
                                r = next_region(nc.tensor)
                                pm = preg[r]
                                cx.wait(nc.tensor, vt)
                                if gi == 0 and ("h2fTw",) in g:
                                    cx.wait(nc.tensor, g[("h2fTw",)])
                                mt = None
                                for j in range(4):
                                    d = gi * 4 + j
                                    mt = nc.tensor.transpose(
                                        pm[:, j * 128:(j + 1) * 128],
                                        sb_f2[:, d * 128:(d + 1) * 128],
                                        c_identf[:])
                                mtok = cx.inc(mt, TQ)
                                cx.wait(nc.vector, mtok)
                                vt2 = cx.inc(nc.vector.tensor_copy(
                                    sb_h2fT[:, gi * 512:(gi + 1) * 512],
                                    pm[:, 0:512]), VQ)
                                g[("reg", r)] = vt2
                            h2T_evac = cx.tok(VQ)
                            if ("h2st", qt % 2) in g:
                                cx.wait(nc.vector, g[("h2st", qt % 2)])
                            vt = cx.inc(nc.vector.tensor_copy(
                                sb_h2rng[:, (qt % 2) * D:(qt % 2 + 1) * D],
                                sb_h2fT[:]), VQ)
                            cx.wait(nc.scalar, vt)
                            h2stok = None
                            for d in range(ND):
                                h2stok = cx.dinc(nc.scalar.dma_start(
                                    out=o_h2T[d * 128:(d + 1) * 128,
                                              qt * 128:(qt + 1) * 128],
                                    in_=sb_h2rng[:, (qt % 2) * D + d * 128:
                                                 (qt % 2) * D + (d + 1) * 128]),
                                    "h2s%d" % (qt % 2))
                            g[("h2st", qt % 2)] = h2stok
                            store_toks.append(h2stok)
                            r = next_region(nc.tensor)
                            pm = preg[r]
                            cx.wait(nc.tensor, h2T_evac)
                            mt = None
                            for d in range(ND):
                                mt = nc.tensor.matmul(
                                    pm[0:8, 0:128],
                                    c_rw[:, d * E:(d + 1) * E],
                                    sb_h2fT[:, d * 128:(d + 1) * 128],
                                    start=(d == 0), stop=(d == ND - 1))
                            mtok = cx.inc(mt, TQ)
                            g[("h2fTw",)] = mtok
                            cx.wait(nc.scalar, mtok)
                            stk = cx.inc(nc.scalar.activation(
                                sb_f0[:, 0:128][0:8, :],
                                pm[0:8, 0:128], AF.Identity,
                                bias=c_rb[:, 0:1], scale=1.0), SQ)
                            g[("reg", r)] = stk
                            r2 = next_region(nc.tensor)
                            pm2 = preg[r2]
                            cx.wait(nc.tensor, stk)
                            mtok = cx.inc(nc.tensor.transpose(
                                pm2[:, 0:8], sb_f0[:, 0:128][0:8, :],
                                c_ident8[:]), TQ)
                            g[("lgf0",)] = mtok
                            cx.wait(nc.vector, mtok)
                            vt2 = cx.inc(nc.vector.tensor_copy(
                                sb_lgtok[:, qt * E:(qt + 1) * E],
                                pm2[:, 0:8]), VQ)
                            g[("reg", r2)] = vt2
                        if debug and hf == 0:
                            cx.wait(nc.gpsimd, qh_done)
                            store_toks.append(cx.dinc(nc.gpsimd.dma_start(out=o_qT2h, in_=sb_qT2h[:]), "dbg"))
                        half_done_t = cx.tok(TQ)
                    b2_done = cx.tok(SQ)

            with ExitStack() as res_:
                pr_a = res_.enter_context(nc.psum_tensor("pr_a", [128, 8], F32))
                pr_b = res_.enter_context(nc.psum_tensor("pr_b", [128, 8], F32))
                pr_c0 = res_.enter_context(nc.psum_tensor("pr_c0", [128, 8], F32))
                pr_c1 = res_.enter_context(nc.psum_tensor("pr_c1", [128, 8], F32))
                pr_t0 = res_.enter_context(nc.psum_tensor("pr_t0", [1, 8], F32))
                pr_t1 = res_.enter_context(nc.psum_tensor("pr_t1", [1, 8], F32))
                sb_t8 = sb("sb_t8", 128, 8, F32)
                sb_e0 = sb("sb_e0", 128, 8, F32)
                sb_e1 = sb("sb_e1", 128, 8, F32)
                sb_oh0 = sb("sb_oh0", 128, 8, F32)
                sb_oh1 = sb("sb_oh1", 128, 8, F32)
                sb_c0 = sb("sb_c0", 128, 8, F32)
                sb_c01 = sb("sb_c01", 128, 8, F32)

                cx.wait(nc.vector, b2_done)
                cx.wait(nc.vector, cx.tok(VQ) if False else cx.tok(VQ))
                for tt in range(NT):
                    lgv = sb_lgtok[:, tt * E:(tt + 1) * E]
                    vw(nc.vector.max(sb_t8[:], lgv))
                    vw(nc.vector.tensor_scalar(sb_e0[:], lgv, sb_t8[:, 0:1],
                                               None, ALU.is_ge))
                    vw(nc.vector.tensor_tensor(sb_e0[:], sb_e0[:], c_iom9[:],
                                               ALU.mult))
                    vw(nc.vector.tensor_scalar_add(sb_e0[:], sb_e0[:], 999.0))
                    vw(nc.vector.tensor_reduce(sb_idx0[:, tt:tt + 1], sb_e0[:],
                                               axis=AX.X, op=ALU.min))
                    vw(nc.vector.tensor_scalar(sb_e0[:], c_iota8[:],
                                               sb_idx0[:, tt:tt + 1], None,
                                               ALU.is_equal))
                    vw(nc.vector.scalar_tensor_tensor(sb_e1[:], sb_e0[:], NEG,
                                                      lgv, ALU.mult, ALU.add))
                    vw(nc.vector.reduce_max(st_[:, 40:41], sb_e1[:], axis=AX.X))
                    vw(nc.vector.tensor_scalar(sb_e1[:], sb_e1[:], st_[:, 40:41],
                                               None, ALU.is_ge))
                    vw(nc.vector.tensor_tensor(sb_e1[:], sb_e1[:], c_iom9[:],
                                               ALU.mult))
                    vw(nc.vector.tensor_scalar_add(sb_e1[:], sb_e1[:], 999.0))
                    vw(nc.vector.tensor_reduce(sb_idx1[:, tt:tt + 1], sb_e1[:],
                                               axis=AX.X, op=ALU.min))
                    vt = vw(nc.vector.tensor_tensor(st_[:, 41:42], st_[:, 40:41],
                                                    sb_t8[:, 0:1], ALU.subtract))
                    cx.wait(nc.scalar, vt)
                    stk = cx.inc(nc.scalar.activation(st_[:, 42:43], st_[:, 41:42],
                                                      AF.Exp), SQ)
                    cx.wait(nc.vector, stk)
                    vw(nc.vector.tensor_scalar_add(st_[:, 43:44], st_[:, 42:43],
                                                   1.0))
                    vw(nc.vector.reciprocal(st_[:, 44:45], st_[:, 43:44]))
                    vw(nc.vector.tensor_copy(sb_sc0[:, tt:tt + 1], st_[:, 44:45]))
                    vw(nc.vector.tensor_tensor(sb_sc1[:, tt:tt + 1], st_[:, 42:43],
                                               st_[:, 44:45], ALU.mult))
                    vw(nc.vector.tensor_scalar(sb_oh0[:], c_iota8[:],
                                               sb_idx0[:, tt:tt + 1], None,
                                               ALU.is_equal))
                    vt = vw(nc.vector.tensor_scalar(
                        sb_oh1[:], c_iota8[:], sb_idx1[:, tt:tt + 1], None,
                        ALU.is_equal))
                    cx.wait(nc.tensor, vt)
                    if tt > 0:
                        cx.wait(nc.tensor, g[("rcu",)])
                    nc.tensor.matmul(pr_c0[:], c_triu[:], sb_oh0[:],
                                     start=True, stop=True)
                    nc.tensor.matmul(pr_c1[:], c_triu[:], sb_oh1[:],
                                     start=True, stop=True)
                    nc.tensor.matmul(pr_t0[:], c_onesc[:], sb_oh0[:],
                                     start=True, stop=True)
                    nc.tensor.matmul(pr_t1[:], c_onesc[:], sb_oh1[:],
                                     start=True, stop=True)
                    nc.tensor.matmul(pr_a[:], c_ones1[:], sb_run0[:],
                                     start=True, stop=True)
                    mtok = cx.inc(nc.tensor.matmul(pr_b[:], c_ones1[:],
                                                   sb_run01[:],
                                                   start=True, stop=True), TQ)
                    cx.wait(nc.vector, mtok)
                    vw(nc.vector.tensor_copy(sb_c0[:], pr_c0[:]))
                    vw(nc.vector.tensor_tensor(sb_c01[:], sb_c0[:], pr_c1[:],
                                               ALU.add))
                    vw(nc.vector.scalar_tensor_tensor(sb_c0[:], sb_c0[:], 1.0,
                                                      pr_a[:], ALU.mult, ALU.add))
                    vw(nc.vector.scalar_tensor_tensor(
                        sb_e0[:], sb_c0[:], 1.0, sb_oh0[:], ALU.mult, ALU.mult,
                        accum_out=sb_cs0[:, tt:tt + 1]))
                    vw(nc.vector.scalar_tensor_tensor(sb_c01[:], sb_c01[:], 1.0,
                                                      pr_b[:], ALU.mult, ALU.add))
                    vw(nc.vector.scalar_tensor_tensor(
                        sb_e1[:], sb_c01[:], 1.0, sb_oh1[:], ALU.mult, ALU.mult,
                        accum_out=sb_cs01[:, tt:tt + 1]))
                    vw(nc.vector.tensor_tensor(sb_run0[:], sb_run0[:], pr_t0[:],
                                               ALU.add))
                    vw(nc.vector.tensor_tensor(sb_run1[:], sb_run1[:], pr_t1[:],
                                               ALU.add))
                    vt = vw(nc.vector.tensor_tensor(
                        sb_run01[:], sb_run0[:], sb_run1[:], ALU.add))
                    g[("rcu",)] = vt
                rt_done = cx.tok(VQ)
                cx.wait(nc.gpsimd, rt_done)
                for dst, srcb in [(o_idx0, sb_idx0), (o_idx1, sb_idx1),
                                  (o_sc0, sb_sc0), (o_sc1, sb_sc1),
                                  (o_cs0, sb_cs0), (o_cs01, sb_cs01)]:
                    store_toks.append(cx.dinc(nc.gpsimd.dma_start(
                        out=dst, in_=srcb[:]), "ro"))
                store_toks.append(cx.dinc(nc.gpsimd.dma_start(
                    out=o_cnt[0:1, :], in_=sb_run0[:]), "ro"))
                store_toks.append(cx.dinc(nc.gpsimd.dma_start(
                    out=o_cnt[1:2, :], in_=sb_run1[:]), "ro"))

                for ch in ("ss0", "ss1", "h2s0", "h2s1", "ro", "dbg"):
                    if ch in cx.cnt:
                        cx.wait(nc.sync, cx.tok(ch))

    return nc


def host_prepare(inputs):
    f32 = np.float32
    X = np.ascontiguousarray(np.asarray(inputs["L0_attn_batch"], f32))
    out = {}
    out["shared"] = sh = {}
    sh["wqT"] = np.ascontiguousarray(np.asarray(inputs["L0_attn_Wq"], f32).T)
    sh["wkT"] = np.ascontiguousarray(np.asarray(inputs["L0_attn_Wk"], f32).T)
    sh["wvT"] = np.ascontiguousarray(np.asarray(inputs["L0_attn_Wv"], f32).T)
    sh["woT"] = np.ascontiguousarray(np.asarray(inputs["L0_attn_Wo"], f32).T)
    sh["bq128"] = np.ascontiguousarray(
        np.asarray(inputs["L0_attn_bq"], f32).reshape(ND, 128).T)
    sh["bk128"] = np.ascontiguousarray(
        np.asarray(inputs["L0_attn_bk"], f32).reshape(ND, 128).T)
    sh["bv_bc"] = np.ascontiguousarray(
        np.broadcast_to(np.asarray(inputs["L0_attn_bv"], f32), (128, D)))
    sh["bo_bc"] = np.ascontiguousarray(
        np.broadcast_to(np.asarray(inputs["L0_attn_bo"], f32), (128, D)))
    sh["g1_bc"] = np.ascontiguousarray(
        np.broadcast_to(np.asarray(inputs["L0_ffn_g1"], f32), (128, D)))
    sh["g2_bc"] = np.ascontiguousarray(
        np.broadcast_to(np.asarray(inputs["L0_ffn_g2"], f32), (128, D)))
    sh["fwq"] = np.ascontiguousarray(np.asarray(inputs["L0_ffn_wq"], f32))
    sh["fwk"] = np.ascontiguousarray(np.asarray(inputs["L0_ffn_wk"], f32))
    sh["fwv"] = np.ascontiguousarray(np.asarray(inputs["L0_ffn_wv"], f32))
    sh["fwo"] = np.ascontiguousarray(np.asarray(inputs["L0_ffn_wo"], f32))
    sh["rw"] = np.ascontiguousarray(np.asarray(inputs["L0_ffn_router_w"], f32))
    sh["rb8"] = np.ascontiguousarray(
        np.asarray(inputs["L0_ffn_router_b"], f32).reshape(E, 1))
    thetas = 10000.0 ** (-np.arange(0, HD, 2, dtype=f32) / HD)
    ticks = np.outer(np.arange(S, dtype=f32), thetas)
    ticks2 = np.tile(ticks, (1, 2))
    cosf = np.cos(ticks2).astype(f32)
    sinf = np.sin(ticks2).astype(f32)
    cosb = np.empty((128, S), f32)
    sinb = np.empty((128, S), f32)
    for p in range(128):
        hd = p % HD
        cosb[p] = cosf[:, hd]
        sgn = -1.0 if hd < HD // 2 else 1.0
        sinb[p] = sgn * sinf[:, hd]
    sh["cosb"] = cosb.astype(BF)
    sh["sinb"] = sinb.astype(BF)
    sh["identb"] = np.eye(128).astype(BF)
    sh["identf"] = np.eye(128, dtype=f32)
    sh["ident8"] = np.eye(8, dtype=f32)
    cm = np.zeros((128, 128), f32)
    for p in range(128):
        cm[p, p + 1:] = NEG
    sh["cmask"] = cm
    sh["triu"] = np.triu(np.ones((128, 128), f32))
    sh["iota8"] = np.broadcast_to(np.arange(8, dtype=f32), (128, 8)).copy()
    sh["iota8m999"] = sh["iota8"] - 999.0
    sh["ones_bf"] = np.ones((128, 1)).astype(BF)
    sh["ones1"] = np.ones((1, 128), f32)
    sh["ones128c"] = np.ones((128, 1), f32)
    out["x"] = [np.ascontiguousarray(X[c]) for c in range(NCORES)]
    return out


def kernel(**inputs):
    hp = host_prepare(inputs)
    nc1 = build_launch1()
    in_maps = []
    for c in range(NCORES):
        m = dict(hp["shared"])
        m["x"] = hp["x"][c]
        in_maps.append(m)
    res1 = run_bass_kernel_spmd(nc1, in_maps, core_ids=list(range(NCORES)))
    raise NotImplementedError("launch 2 pending")


def build_launch2():
    nc = bass.Bass("TRN2", target_bir_lowering=False)

    table_d = nc.dram_tensor("table", [2 * NCORES, E], F32, kind="ExternalInput").ap()
    pmask_d = nc.dram_tensor("pmask", [2 * NCORES, 2], F32, kind="ExternalInput").ap()
    s_d = nc.dram_tensor("s_in", [S, D], BF16, kind="ExternalInput").ap()
    h2T_d = nc.dram_tensor("h2T_in", [D, S], BF16, kind="ExternalInput").ap()
    idx0_d = nc.dram_tensor("idx0", [128, NT], F32, kind="ExternalInput").ap()
    idx1_d = nc.dram_tensor("idx1", [128, NT], F32, kind="ExternalInput").ap()
    sc0_d = nc.dram_tensor("sc0", [128, NT], F32, kind="ExternalInput").ap()
    sc1_d = nc.dram_tensor("sc1", [128, NT], F32, kind="ExternalInput").ap()
    cs0_d = nc.dram_tensor("cs0", [128, NT], F32, kind="ExternalInput").ap()
    cs01_d = nc.dram_tensor("cs01", [128, NT], F32, kind="ExternalInput").ap()
    mw1_d = nc.dram_tensor("mw1", [E * D, HID], F32, kind="ExternalInput").ap()
    mw2_d = nc.dram_tensor("mw2", [E * D, HID], F32, kind="ExternalInput").ap()
    mw3_d = nc.dram_tensor("mw3", [E * HID, D], F32, kind="ExternalInput").ap()
    iota8_d = nc.dram_tensor("iota8", [128, E], F32, kind="ExternalInput").ap()
    identb_d = nc.dram_tensor("identb", [128, 128], BF16, kind="ExternalInput").ap()
    ones1_d = nc.dram_tensor("ones1", [1, 128], F32, kind="ExternalInput").ap()
    o_out = nc.dram_tensor("o_out", [S, D], F32, kind="ExternalOutput").ap()

    es = ExitStack()
    cx = Ctx(nc, es)

    def sb(name, p, c, dt=F32):
        return es.enter_context(nc.sbuf_tensor(name, [p, c], dt))

    c_iota8 = sb("c_iota8", 128, E, F32)
    c_identb = sb("c_identb", 128, 128, BF16)
    c_ones1 = sb("c_ones1", 1, 128, F32)
    c_eps5 = sb("c_eps5", 128, 1, F32)
    sb_tab = sb("sb_tab", 2 * NCORES, E, F32)
    sb_pm = sb("sb_pm", 2 * NCORES, 2, F32)
    sb_b0r = sb("sb_b0r", 1, E, F32)
    sb_b01r = sb("sb_b01r", 1, E, F32)
    sb_b0bc = sb("sb_b0bc", 128, E, F32)
    sb_b01bc = sb("sb_b01bc", 128, E, F32)
    sb_i0 = sb("sb_i0", 128, NT, F32)
    sb_i1 = sb("sb_i1", 128, NT, F32)
    sb_s0 = sb("sb_s0", 128, NT, F32)
    sb_s1 = sb("sb_s1", 128, NT, F32)
    sb_c0 = sb("sb_c0", 128, NT, F32)
    sb_c01 = sb("sb_c01", 128, NT, F32)
    sb_bt = sb("sb_bt", 128, NT, F32)
    sb_ae = sb("sb_ae", 128, E * NT, F32)
    sb_oh = sb("sb_oh", 128, E, F32)
    sb_sc = sb("sb_sc", 128, E, F32)
    st_ = sb("st2_", 128, 16, F32)
    sb_h2T = sb("sb_h2T", 128, ND * S, BF16)
    sb_acc = sb("sb_acc", 128, NT * D, F32)

    TQ2, VQ2, SQ2 = "tq", "vq", "sq"

    with es:
        with nc.Block():
            g = {}

            def vw(instr):
                tok = cx.inc(instr, VQ2)
                cx.wait(nc.vector, tok)
                return tok

            for dst, srcd in [(c_iota8, iota8_d), (c_identb, identb_d),
                              (c_ones1, ones1_d), (sb_tab, table_d),
                              (sb_pm, pmask_d), (sb_i0, idx0_d), (sb_i1, idx1_d),
                              (sb_s0, sc0_d), (sb_s1, sc1_d), (sb_c0, cs0_d),
                              (sb_c01, cs01_d)]:
                cx.dinc(nc.sync.dma_start(out=dst[:], in_=srcd), "cld")
            for d in range(ND):
                cx.dinc(nc.sync.dma_start(
                    out=sb_h2T[:, d * S:(d + 1) * S],
                    in_=h2T_d[d * 128:(d + 1) * 128, :]), "cld")
            consts_tok = cx.tok("cld")
            for eng in (nc.vector, nc.scalar, nc.tensor, nc.gpsimd):
                cx.wait(eng, consts_tok)
            nc.vector.memset(c_eps5[:], 1.0e-5)

            with ExitStack() as rsc:
                pb0 = rsc.enter_context(nc.psum_tensor("pb0", [1, E], F32))
                pb01 = rsc.enter_context(nc.psum_tensor("pb01", [1, E], F32))
                pbc = rsc.enter_context(nc.psum_tensor("pbc", [128, E], F32))
                pbc2 = rsc.enter_context(nc.psum_tensor("pbc2", [128, E], F32))
                nc.tensor.matmul(pb0[:], sb_pm[:, 0:1], sb_tab[:],
                                 start=True, stop=True)
                mtok = cx.inc(nc.tensor.matmul(pb01[:], sb_pm[:, 1:2], sb_tab[:],
                                               start=True, stop=True), TQ2)
                cx.wait(nc.vector, mtok)
                vw(nc.vector.tensor_copy(sb_b0r[:], pb0[:]))
                vt = vw(nc.vector.tensor_copy(sb_b01r[:], pb01[:]))
                cx.wait(nc.tensor, vt)
                nc.tensor.matmul(pbc[:], c_ones1[:], sb_b0r[:],
                                 start=True, stop=True)
                mtok = cx.inc(nc.tensor.matmul(pbc2[:], c_ones1[:], sb_b01r[:],
                                               start=True, stop=True), TQ2)
                cx.wait(nc.vector, mtok)
                vw(nc.vector.tensor_copy(sb_b0bc[:], pbc[:]))
                vw(nc.vector.tensor_copy(sb_b01bc[:], pbc2[:]))

                for tt in range(NT):
                    vw(nc.vector.tensor_scalar(sb_oh[:], c_iota8[:],
                                               sb_i0[:, tt:tt + 1], None,
                                               ALU.is_equal))
                    vw(nc.vector.scalar_tensor_tensor(
                        sb_sc[:], sb_oh[:], 1.0, sb_b0bc[:], ALU.mult, ALU.mult,
                        accum_out=st_[:, 0:1]))
                    vw(nc.vector.tensor_tensor(st_[:, 1:2], st_[:, 0:1],
                                               sb_c0[:, tt:tt + 1], ALU.add))
                    vw(nc.vector.tensor_scalar(st_[:, 2:3], st_[:, 1:2], CAP,
                                               None, ALU.is_lt))
                    vw(nc.vector.tensor_scalar(sb_oh[:], c_iota8[:],
                                               sb_i1[:, tt:tt + 1], None,
                                               ALU.is_equal))
                    vw(nc.vector.scalar_tensor_tensor(
                        sb_sc[:], sb_oh[:], 1.0, sb_b01bc[:], ALU.mult, ALU.mult,
                        accum_out=st_[:, 3:4]))
                    vw(nc.vector.tensor_tensor(st_[:, 4:5], st_[:, 3:4],
                                               sb_c01[:, tt:tt + 1], ALU.add))
                    vw(nc.vector.tensor_scalar(st_[:, 5:6], st_[:, 4:5], CAP,
                                               None, ALU.is_lt))
                    vw(nc.vector.tensor_tensor(st_[:, 6:7], sb_s0[:, tt:tt + 1],
                                               st_[:, 2:3], ALU.mult))
                    vw(nc.vector.scalar_tensor_tensor(
                        st_[:, 7:8], sb_s1[:, tt:tt + 1], 1.0, st_[:, 5:6],
                        ALU.mult, ALU.mult))
                    vw(nc.vector.tensor_tensor(st_[:, 8:9], st_[:, 6:7],
                                               st_[:, 7:8], ALU.add))
                    vw(nc.vector.tensor_tensor(st_[:, 9:10], sb_s0[:, tt:tt + 1],
                                               sb_s1[:, tt:tt + 1], ALU.add))
                    vw(nc.vector.tensor_tensor(sb_bt[:, tt:tt + 1], st_[:, 9:10],
                                               st_[:, 8:9], ALU.subtract))
                    vw(nc.vector.tensor_tensor(st_[:, 10:11], sb_i0[:, tt:tt + 1],
                                               st_[:, 2:3], ALU.mult))
                    vw(nc.vector.tensor_tensor(st_[:, 11:12], sb_i1[:, tt:tt + 1],
                                               st_[:, 5:6], ALU.mult))
                    vw(nc.vector.tensor_tensor(st_[:, 12:13], st_[:, 10:11],
                                               st_[:, 11:12], ALU.max))
                    for e in range(E):
                        vw(nc.vector.tensor_scalar(
                            sb_ae[:, e * NT + tt:e * NT + tt + 1],
                            st_[:, 12:13], float(e), st_[:, 8:9],
                            ALU.is_equal, ALU.mult))
                rt_done = cx.tok(VQ2)

            with ExitStack() as msc:
                pt12a = msc.enter_context(nc.psum_tensor("pt12a", [128, 512], F32))
                pt12b = msc.enter_context(nc.psum_tensor("pt12b", [128, 512], F32))
                py0 = msc.enter_context(nc.psum_tensor("py0", [128, 1024], F32))
                py1 = msc.enter_context(nc.psum_tensor("py1", [128, 1024], F32))
                fw1 = msc.enter_context(nc.sbuf_tensor("fw1", [128, 2 * ND * 1024], BF16))
                fw2 = msc.enter_context(nc.sbuf_tensor("fw2", [128, 2 * ND * 1024], BF16))
                fw3 = msc.enter_context(nc.sbuf_tensor("fw3", [128, 2 * ND * 1024], BF16))
                wstag = msc.enter_context(nc.sbuf_tensor("wstag", [128, 1024], F32))
                sb_t1 = msc.enter_context(nc.sbuf_tensor("sb_t1", [128, 256], F32))
                sb_pr = msc.enter_context(nc.sbuf_tensor("sb_pr", [128, 256], F32))
                sb_hs = msc.enter_context(nc.sbuf_tensor("sb_hs", [128, 2 * 256], BF16))
                sb_ytmp = msc.enter_context(nc.sbuf_tensor("sb_ytmp", [128, 2 * D], BF16))

                def load_whalf(e, hh, slot, war):
                    alltoks = []
                    for mat, dram, buf in ((1, mw1_d, fw1), (2, mw2_d, fw2),
                                           (3, mw3_d, fw3)):
                        for d in range(ND):
                            for cc in range(2):
                                k = len(alltoks)
                                pp = k % 2
                                if k >= 2:
                                    cx.wait(nc.sync, alltoks[k - 2])
                                elif war:
                                    for w_ in war:
                                        cx.wait(nc.sync, w_)
                                if mat == 3:
                                    srcap = dram[e * HID + hh * 1024 + d * 128:
                                                 e * HID + hh * 1024 + (d + 1) * 128,
                                                 cc * 512:(cc + 1) * 512]
                                else:
                                    srcap = dram[e * D + d * 128:
                                                 e * D + (d + 1) * 128,
                                                 hh * 1024 + cc * 512:
                                                 hh * 1024 + (cc + 1) * 512]
                                t = cx.dinc(nc.sync.dma_start(
                                    out=wstag[:, pp * 512:(pp + 1) * 512],
                                    in_=srcap), "mw%d" % pp)
                                cx.wait(nc.gpsimd, t)
                                if k < 2 and war:
                                    for w_ in war:
                                        cx.wait(nc.gpsimd, w_)
                                dst_ = buf[:, slot * 8192 + d * 1024 + cc * 512:
                                           slot * 8192 + d * 1024 + (cc + 1) * 512]
                                src_ = wstag[:, pp * 512:(pp + 1) * 512]
                                tok = cx.inc(nc.gpsimd.tensor_copy(dst_, src_),
                                             "gq")
                                alltoks.append(tok)
                    return alltoks[-1], alltoks[-1]

                phases = [(e, hh) for e in range(E) for hh in range(2)]
                wtoks = {}
                wtoks[0] = load_whalf(0, 0, 0, None)
                cx.wait(nc.scalar, rt_done)
                for pi, (e, hh) in enumerate(phases):
                    slot = pi % 2
                    if pi + 1 < len(phases):
                        en, hn = phases[pi + 1]
                        war = ([g[("wslot", (pi + 1) % 2)]]
                               if ("wslot", (pi + 1) % 2) in g else None)
                        wtoks[pi + 1] = load_whalf(en, hn, (pi + 1) % 2, war)
                    vtk, stk_ = wtoks[pi]
                    cx.wait(nc.tensor, vtk)
                    cx.wait(nc.tensor, stk_)
                    first = (pi == 0)

                    def emit_y(ts, ht):
                        par = ht % 2
                        cx.wait(nc.tensor, g[("gel", par)])
                        if ht == 0:
                            for half in range(2):
                                if ("pyev", half) in g:
                                    cx.wait(nc.tensor, g[("pyev", half)])
                        mt = None
                        for half in range(2):
                            pyp = py0 if half == 0 else py1
                            for c2 in range(2):
                                mt = nc.tensor.matmul(
                                    pyp[:, c2 * 512:(c2 + 1) * 512],
                                    sb_hs[:, par * 256 + half * 128:
                                          par * 256 + (half + 1) * 128],
                                    fw3[:, slot * 8192 + ht * 1024 + c2 * 512:
                                        slot * 8192 + ht * 1024 + (c2 + 1) * 512],
                                    start=(ht == 0), stop=(ht == ND - 1))
                        tok = cx.inc(mt, TQ2)
                        g[("ymm", par)] = tok
                        if ht == ND - 1:
                            g[("ylast",)] = tok

                    def emit_pyevac(ts):
                        pp = ts % 2
                        cx.wait(nc.scalar, g[("ylast",)])
                        for half in range(2):
                            if ("ytmp", half) in g:
                                cx.wait(nc.scalar, g[("ytmp", half)])
                            tt = ts * 2 + half
                            pyp = py0 if half == 0 else py1
                            aec = sb_ae[:, e * NT + tt:e * NT + tt + 1]
                            sctk = cx.inc(nc.scalar.activation(
                                sb_ytmp[:, half * D:(half + 1) * D],
                                pyp[:], AF.Identity, scale=aec), SQ2)
                            g[("pyev", half)] = sctk
                            cx.wait(nc.vector, sctk)
                            acct = sb_acc[:, tt * D:(tt + 1) * D]
                            ysrc = sb_ytmp[:, half * D:(half + 1) * D]
                            if first:
                                atk = cx.inc(nc.vector.tensor_copy(acct, ysrc),
                                             VQ2)
                            else:
                                atk = cx.inc(nc.vector.tensor_tensor(
                                    acct, acct, ysrc, ALU.add), VQ2)
                            g[("ytmp", half)] = atk

                    pend = None
                    pend_evac = None
                    for ts in range(8):
                        for ht in range(8):
                            par = ht % 2
                            pt = pt12a if par == 0 else pt12b
                            if ("t12", par) in g:
                                cx.wait(nc.tensor, g[("t12", par)])
                            mt = None
                            for d in range(ND):
                                mt = nc.tensor.matmul(
                                    pt[:, 0:256],
                                    fw1[:, slot * 8192 + d * 1024 + ht * 128:
                                        slot * 8192 + d * 1024 + (ht + 1) * 128],
                                    sb_h2T[:, d * S + ts * 256:
                                           d * S + (ts + 1) * 256],
                                    start=(d == 0), stop=(d == ND - 1))
                            for d in range(ND):
                                mt = nc.tensor.matmul(
                                    pt[:, 256:512],
                                    fw2[:, slot * 8192 + d * 1024 + ht * 128:
                                        slot * 8192 + d * 1024 + (ht + 1) * 128],
                                    sb_h2T[:, d * S + ts * 256:
                                           d * S + (ts + 1) * 256],
                                    start=(d == 0), stop=(d == ND - 1))
                            mtok = cx.inc(mt, TQ2)
                            if pend is not None:
                                emit_y(*pend)
                            if pend_evac is not None and ht == 0:
                                emit_pyevac(pend_evac)
                                pend_evac = None
                            pend = (ts, ht)
                            cx.wait(nc.scalar, mtok)
                            sct = cx.inc(nc.scalar.copy(sb_t1[:], pt[:, 0:256]),
                                         SQ2)
                            cx.wait(nc.vector, sct)
                            prt = cx.inc(nc.vector.scalar_tensor_tensor(
                                sb_pr[:], sb_t1[:], 1.0, pt[:, 256:512],
                                ALU.mult, ALU.mult), VQ2)
                            g[("t12", par)] = prt
                            cx.wait(nc.scalar, prt)
                            if ("ymm", par) in g:
                                cx.wait(nc.scalar, g[("ymm", par)])
                            gtk = cx.inc(nc.scalar.activation(
                                sb_hs[:, par * 256:(par + 1) * 256], sb_pr[:],
                                AF.Gelu), SQ2)
                            g[("gel", par)] = gtk
                        if pend is not None and pend[1] == ND - 1 and ts == 7:
                            emit_y(*pend)
                            pend = None
                            emit_pyevac(ts)
                        else:
                            pend_evac_prev = pend_evac
                            pend_evac = ts if pend is not None else pend_evac
                    if pend is not None:
                        emit_y(*pend)
                        pend = None
                    if pend_evac is not None and pend_evac != 7:
                        emit_pyevac(pend_evac)
                        pend_evac = None
                    g[("wslot", slot)] = cx.tok(TQ2)
                moe_done_v = cx.tok(VQ2)

            with ExitStack() as fsc:
                ptr = fsc.enter_context(nc.psum_tensor("ptr", [128, 512], BF16))
                ptr2 = fsc.enter_context(nc.psum_tensor("ptr2", [128, 512], BF16))
                sb_h2n = fsc.enter_context(nc.sbuf_tensor("sb_h2n", [128, D], BF16))
                sb_sin = fsc.enter_context(nc.sbuf_tensor("sb_sin", [128, 2 * D], BF16))
                sb_pf = fsc.enter_context(nc.sbuf_tensor("sb_pf", [128, D], F32))
                sb_sq = fsc.enter_context(nc.sbuf_tensor("sb_sq2", [128, D], F32))
                sb_of = fsc.enter_context(nc.sbuf_tensor("sb_of", [128, 2 * D], F32))
                stoks = [None] * NT
                moe_done_t = cx.tok(TQ2)
                moe_done_s = cx.tok(SQ2)
                cx.wait(nc.sync, moe_done_t)
                cx.wait(nc.sync, moe_done_v)
                cx.wait(nc.sync, moe_done_s)
                for i in range(2):
                    stoks[i] = cx.dinc(nc.sync.dma_start(
                        out=sb_sin[:, (i % 2) * D:(i % 2 + 1) * D],
                        in_=s_d[i * 128:(i + 1) * 128, :]), "sl%d" % (i % 2))
                cx.wait(nc.tensor, moe_done_v)
                cx.wait(nc.tensor, moe_done_s)
                out_toks = []
                for tt in range(NT):
                    for gi in range(2):
                        pt = ptr if gi % 2 == 0 else ptr2
                        if ("ftr", gi % 2) in g:
                            cx.wait(nc.tensor, g[("ftr", gi % 2)])
                        mt = None
                        for j in range(4):
                            d = gi * 4 + j
                            mt = nc.tensor.transpose(
                                pt[:, j * 128:(j + 1) * 128],
                                sb_h2T[:, d * S + tt * 128:d * S + (tt + 1) * 128],
                                c_identb[:])
                        mtok = cx.inc(mt, TQ2)
                        cx.wait(nc.scalar, mtok)
                        stk2 = nc.scalar.copy(
                            sb_h2n[:, gi * 512:(gi + 1) * 512], pt[:])
                        g[("ftr", gi % 2)] = cx.inc(stk2, SQ2)
                    h2n_done = cx.tok(SQ2)
                    cx.wait(nc.vector, h2n_done)
                    cx.wait(nc.vector, stoks[tt])
                    vw(nc.vector.scalar_tensor_tensor(
                        sb_pf[:], sb_h2n[:], sb_bt[:, tt:tt + 1],
                        sb_acc[:, tt * D:(tt + 1) * D], ALU.mult, ALU.add))
                    sfree = vw(nc.vector.tensor_tensor(
                        sb_pf[:], sb_pf[:],
                        sb_sin[:, (tt % 2) * D:(tt % 2 + 1) * D], ALU.add))
                    if tt + 2 < NT:
                        cx.wait(nc.sync, sfree)
                        stoks[tt + 2] = cx.dinc(nc.sync.dma_start(
                            out=sb_sin[:, ((tt + 2) % 2) * D:((tt + 2) % 2 + 1) * D],
                            in_=s_d[(tt + 2) * 128:(tt + 3) * 128, :]),
                            "sl%d" % (tt % 2))
                    vw(nc.vector.reduce_sum(st_[:, 13:14], sb_pf[:], axis=AX.X))
                    vw(nc.vector.tensor_scalar_mul(st_[:, 14:15], st_[:, 13:14],
                                                   1.0 / D))
                    vw(nc.vector.tensor_scalar(sb_pf[:], sb_pf[:], st_[:, 14:15],
                                               None, ALU.subtract))
                    vt = cx.inc(nc.vector.scalar_tensor_tensor(
                        sb_sq[:], sb_pf[:], 1.0, sb_pf[:], ALU.mult, ALU.mult,
                        accum_out=st_[:, 15:16]), VQ2)
                    cx.wait(nc.scalar, vt)
                    stk2 = cx.inc(nc.scalar.activation(
                        st_[:, 13:14], st_[:, 15:16], AF.Sqrt,
                        bias=c_eps5[:, 0:1], scale=1.0 / D), SQ2)
                    cx.wait(nc.vector, stk2)
                    vw(nc.vector.reciprocal(st_[:, 14:15], st_[:, 13:14]))
                    if ("ost", tt % 2) in g:
                        cx.wait(nc.vector, g[("ost", tt % 2)])
                    vt = cx.inc(nc.vector.tensor_scalar_mul(
                        sb_of[:, (tt % 2) * D:(tt % 2 + 1) * D], sb_pf[:],
                        st_[:, 14:15]), VQ2)
                    cx.wait(nc.gpsimd, vt)
                    otk = cx.dinc(nc.gpsimd.dma_start(
                        out=o_out[tt * 128:(tt + 1) * 128, :],
                        in_=sb_of[:, (tt % 2) * D:(tt % 2 + 1) * D]),
                        "os%d" % (tt % 2))
                    g[("ost", tt % 2)] = otk
                    out_toks.append(otk)
                for ch in ("os0", "os1"):
                    if ch in cx.cnt:
                        cx.wait(nc.sync, cx.tok(ch))
    return nc


def host_prepare2(inputs):
    f32 = np.float32
    sh = {}
    sh["mw1"] = np.ascontiguousarray(
        np.asarray(inputs["L0_ffn_w1"], f32).reshape(E * D, HID))
    sh["mw2"] = np.ascontiguousarray(
        np.asarray(inputs["L0_ffn_w2"], f32).reshape(E * D, HID))
    sh["mw3"] = np.ascontiguousarray(
        np.asarray(inputs["L0_ffn_w3"], f32).reshape(E * HID, D))
    sh["iota8"] = np.broadcast_to(np.arange(E, dtype=f32), (128, E)).copy()
    sh["identb"] = np.eye(128).astype(BF)
    sh["ones1"] = np.ones((1, 128), f32)
    pms = []
    for c in range(NCORES):
        m = np.zeros((2 * NCORES, 2), f32)
        for cp in range(c):
            m[2 * cp, 0] = 1.0
            m[2 * cp, 1] = 1.0
            m[2 * cp + 1, 1] = 1.0
        pms.append(m)
    return sh, pms


def kernel(**inputs):
    hp = host_prepare(inputs)
    nc1 = build_launch1()
    in_maps = []
    for c in range(NCORES):
        m = dict(hp["shared"])
        m["x"] = hp["x"][c]
        in_maps.append(m)
    res1 = run_bass_kernel_spmd(nc1, in_maps, core_ids=list(range(NCORES)))

    table = np.ascontiguousarray(np.concatenate(
        [np.asarray(res1.results[c]["o_cnt"]) for c in range(NCORES)], axis=0))
    sh2, pms = host_prepare2(inputs)
    nc2 = build_launch2()
    in_maps2 = []
    for c in range(NCORES):
        r = res1.results[c]
        m = dict(sh2)
        m["table"] = table
        m["pmask"] = pms[c]
        m["s_in"] = np.asarray(r["o_s"])
        m["h2T_in"] = np.asarray(r["o_h2T"])
        m["idx0"] = np.asarray(r["o_idx0"])
        m["idx1"] = np.asarray(r["o_idx1"])
        m["sc0"] = np.asarray(r["o_sc0"])
        m["sc1"] = np.asarray(r["o_sc1"])
        m["cs0"] = np.asarray(r["o_csel0"])
        m["cs01"] = np.asarray(r["o_csel01"])
        in_maps2.append(m)
    res2 = run_bass_kernel_spmd(nc2, in_maps2, core_ids=list(range(NCORES)))
    out = np.stack([np.asarray(res2.results[c]["o_out"]) for c in range(NCORES)],
                   axis=0)
    return out
